# version 07
# baseline (speedup 1.0000x reference)
import numpy as np
import concourse.bass as bass
import concourse.mybir as mybir
from concourse.bass_utils import run_bass_kernel_spmd

F32 = mybir.dt.float32
BF16 = mybir.dt.bfloat16
F16 = mybir.dt.float16
ALU = mybir.AluOpType
AF = mybir.ActivationFunctionType
AX = mybir.AxisListType.X

P = 128
CC = 8
SL = 512
NS = 8
TOK = NS * SL
MH = 32
MAGIC = float(np.float32(12582912.0))


class Sem:
    def __init__(self, nc, name):
        self.h = nc.alloc_semaphore(name)
        self.c = 0


class B:
    ENGS = ("sync", "vector", "scalar", "tensor", "gpsimd")

    def __init__(self, nc):
        self.nc = nc
        self.ops = {e: [] for e in self.ENGS}

    def wait(self, eng, sem, val=None):
        v = sem.c if val is None else val
        assert v <= sem.c, f"wait on future value {v} > {sem.c}"
        self.ops[eng].append(lambda e, s=sem, v=v: e.wait_ge(s.h, v))

    def op(self, eng, fn, inc=None, inc_n=1):
        if inc is not None:
            inc.c += inc_n
        def run(e, fn=fn, inc=inc, inc_n=inc_n):
            ins = fn(e)
            if inc is not None:
                ins.then_inc(inc.h, inc_n)
        self.ops[eng].append(run)

    def drain(self, eng, sem=None):
        self.op(eng, lambda e: e.drain(), inc=sem)

    def dma(self, eng, out, in_, sem):
        self.op(eng, lambda e, o=out, i=in_: e.dma_start(out=o, in_=i), inc=sem, inc_n=16)


def build_nc(dbg=False):
    nc = bass.Bass("TRN2", target_bir_lowering=False)

    dx = nc.dram_tensor("x", [1024, TOK], F32, kind="ExternalInput").ap()
    dxprev = nc.dram_tensor("xprev", [P, CC], F32, kind="ExternalInput").ap()
    dmask = nc.dram_tensor("mask", [P, CC], F32, kind="ExternalInput").ap()
    dwk = nc.dram_tensor("wk", [CC, P, CC, P], F32, kind="ExternalInput").ap()
    dwv = nc.dram_tensor("wv", [CC, P, CC, P], F32, kind="ExternalInput").ap()
    dwr = nc.dram_tensor("wr", [CC, P, CC, P], F32, kind="ExternalInput").ap()
    dwo = nc.dram_tensor("wo", [CC, P, CC, P], F32, kind="ExternalInput").ap()
    dw1 = nc.dram_tensor("w1", [MH, P, CC, P], F32, kind="ExternalInput").ap()
    dw2 = nc.dram_tensor("w2", [MH, P, CC, P], F32, kind="ExternalInput").ap()
    dh1 = nc.dram_tensor("h1", [MH, P, MH, P], F32, kind="ExternalInput").ap()
    dh2 = nc.dram_tensor("h2", [MH, P, MH, P], F32, kind="ExternalInput").ap()
    dw3 = nc.dram_tensor("w3", [CC, P, MH, P], F32, kind="ExternalInput").ap()
    vec_names = ["tmk", "tmv", "tmr", "rmsk", "rmsv", "rmsr", "rmso", "td", "tf"]
    dvecs = {n: nc.dram_tensor(n, [P, CC], F32, kind="ExternalInput").ap() for n in vec_names}
    ds1 = nc.dram_tensor("s1", [P, MH], F32, kind="ExternalInput").ap()
    ds2 = nc.dram_tensor("s2", [P, MH], F32, kind="ExternalInput").ap()
    dsh1 = nc.dram_tensor("sh1", [P, MH], F32, kind="ExternalInput").ap()
    dsh2 = nc.dram_tensor("sh2", [P, MH], F32, kind="ExternalInput").ap()
    ds3 = nc.dram_tensor("s3", [P, CC], F32, kind="ExternalInput").ap()
    dy = nc.dram_tensor("y", [1024, TOK], F32, kind="ExternalOutput").ap()

    ident_h = nc.inline_tensor(np.eye(P, dtype=np.float32), name="c_ident")
    oc_h = nc.inline_tensor(np.concatenate([np.ones((1, P), np.float32),
                                            np.full((1, P), 127.0, np.float32)], axis=1), name="c_oc")
    zero_h = nc.inline_tensor(np.zeros((P, 1), np.float32), name="c_zero")
    eps_h = nc.inline_tensor(np.tile(np.array([[1e-5, 1e-8]], np.float32), (P, 1)), name="c_eps")
    onec_h = nc.inline_tensor(np.ones((P, 1), np.float32), name="c_onec")

    wqo_h = nc.dram_tensor("wqo_s", [CC, P, CC, P], BF16, kind="Internal").ap()
    w1b = nc.dram_tensor("w1b", [MH, P, CC, P], BF16, kind="Internal").ap()
    w2b = nc.dram_tensor("w2b", [MH, P, CC, P], BF16, kind="Internal").ap()
    h1b = nc.dram_tensor("h1b", [MH, P, MH, P], BF16, kind="Internal").ap()
    h2b = nc.dram_tensor("h2b", [MH, P, MH, P], BF16, kind="Internal").ap()
    w3b = nc.dram_tensor("w3b", [CC, P, MH, P], BF16, kind="Internal").ap()
    sp_n = nc.dram_tensor("sp_n", [NS, P, CC * SL], F16, kind="Internal").ap()
    sp_d = nc.dram_tensor("sp_d", [NS, P, CC * SL], F16, kind="Internal").ap()
    sp_r = nc.dram_tensor("sp_r", [NS, P, CC * SL], F16, kind="Internal").ap()
    cc_in = nc.dram_tensor("cc_in", [P, 16], F32, kind="Internal").ap()
    cc_out = nc.dram_tensor("cc_out", [2 * P, 16], F32, kind="Internal").ap()

    BASE = (nc.sbuf_base + 63) // 64 * 64

    def at(name, shape, dtype, off):
        return nc.alloc_sbuf_tensor_at(name, shape, dtype, offset=BASE + off)

    X0 = at("X0", [P, CC, SL], F32, 0)
    Zb = at("Zb", [P, CC, SL], BF16, 16384)
    F0 = at("F0", [P, CC, SL], F16, 24576)
    F1 = at("F1", [P, CC, SL], F16, 32768)
    F2 = at("F2", [P, CC, SL], F16, 40960)
    ZK = at("ZK", [P, CC, SL], BF16, 24576)
    ZV = at("ZV", [P, CC, SL], BF16, 32768)
    ZR = at("ZR", [P, CC, SL], BF16, 40960)
    H = at("H", [P, CC, SL + 1], BF16, 49152)
    DP = at("DP", [P, CC, SL], F16, 57408)
    RB = 65600
    A0 = at("A0", [P, CC, SL + 1], F32, RB)
    A1 = at("A1", [P, CC, SL + 1], F32, RB + 16416)
    A2 = at("A2", [P, CC, SL], F32, RB + 32832)
    A3 = at("A3", [P, CC, SL], F32, RB + 49216)
    A4 = at("A4", [P, CC, SL], F32, RB + 65600)
    WQK = at("WQK", [P, CC, 1024], BF16, RB + 81984)
    WQV = at("WQV", [P, CC, 1024], BF16, RB + 98368)
    WQR = at("WQR", [P, CC, 1024], BF16, RB + 114752)
    PRW = at("PRW", [P, CC, 1024], F32, RB)
    PRS = at("PRS", [P, 1024], F32, RB + 49216)
    PRB = at("PRB", [P, 1024], BF16, RB + 65600)
    HB = at("HB", [P, MH, SL], BF16, RB)
    H2B = at("H2B", [P, MH, SL], BF16, RB + 32768)
    SQB2 = at("SQB2", [P, CC, SL], F32, RB)
    FH0 = at("FH0", [P, MH, P], BF16, RB + 65536)
    FH1 = at("FH1", [P, MH, P], BF16, RB + 73728)
    FW0 = at("FW0", [P, CC, P], BF16, RB + 81920)
    FW1 = at("FW1", [P, CC, P], BF16, RB + 83968)
    FW2 = at("FW2", [P, CC, P], BF16, RB + 86016)
    FW3 = at("FW3", [P, CC, P], BF16, RB + 88064)
    YF = at("YF", [P, CC, SL], F32, RB + 90112)
    QS = at("QS", [P, CC, SL], F32, RB + 106496)
    SM = 196736
    U0 = at("U0", [P, SL], F32, SM)
    U1 = at("U1", [P, SL], F32, SM + 2048)
    DBC = at("DBC", [P, SL], F32, SM + 4096)
    IDE = at("IDE", [P, P], F32, SM + 6144)
    FROW = at("FROW", [1, CC * P], F32, SM + 6656)
    ABLK = at("ABLK", [P, SL], F32, SM + 6656)
    smo = SM + 10752

    def small(name, shape, dtype):
        nonlocal smo
        nbytes = int(np.prod(shape[1:])) * (4 if dtype == F32 else 2)
        t = at(name, shape, dtype, smo)
        smo += (nbytes + 63) // 64 * 64
        return t

    V = {n: small("v_" + n, [P, CC], F32) for n in vec_names}
    Dt = small("Dt", [P, CC], F32)
    Rd = small("Rd", [P, CC], F32)
    Eu = small("Eu", [P, CC], F32)
    C1 = small("C1", [P, CC], F32)
    Msk = small("Msk", [P, CC], F32)
    Xpv = small("Xpv", [P, CC], F32)
    Pin = small("Pin", [P, CC], F32)
    Qin = small("Qin", [P, CC], F32)
    Pcs = small("Pcs", [P, CC], F32)
    Qcs = small("Qcs", [P, CC], F32)
    PiT = small("PiT", [P, CC], F32)
    QiT = small("QiT", [P, CC], F32)
    Gio = small("Gio", [P, 16], F32)
    Gin = small("Gin", [P, 16], F32)
    Tok0 = small("Tok0", [P, 4], F32)
    Tok1 = small("Tok1", [P, 4], F32)
    Fd = small("Fd", [P, 8], F32)
    OC = small("OC", [1, 2 * P], F32)
    Zc = small("Zc", [P, 1], F32)
    ISW = {n: small("isw_" + n, [P, 1], F32) for n in ("k", "v", "r", "o")}
    SWB = small("SWB", [P, 1], F32)
    Sc1 = small("Sc1", [1, 4], F32)
    S1t = small("S1t", [P, MH], F32)
    S2t = small("S2t", [P, MH], F32)
    Sh1t = small("Sh1t", [P, MH], F32)
    Sh2t = small("Sh2t", [P, MH], F32)
    S3t = small("S3t", [P, CC], F32)
    CB = small("CB", [P, 2], F32)
    ONEC = small("ONEC", [P, 1], F32)
    assert BASE + smo <= nc.sbuf_top, (BASE, smo)

    PK = nc.alloc_psum_tensor("PK", [P, 4, SL], F32)
    PB0 = nc.alloc_psum_tensor("PB0", [P, SL], F32)
    PB1 = nc.alloc_psum_tensor("PB1", [P, SL], F32)
    PT = nc.alloc_psum_tensor("PT", [P, 2, SL], F32)

    b = B(nc)
    s_x = Sem(nc, "s_x")
    s_w = Sem(nc, "s_w")
    s_cast = Sem(nc, "s_cast")
    s_fh = Sem(nc, "s_fh")
    s_fw = Sem(nc, "s_fw")
    s_rl = Sem(nc, "s_rl")
    s_pw = Sem(nc, "s_pw")
    s_sp = Sem(nc, "s_sp")
    s_out = Sem(nc, "s_out")
    s_cc = Sem(nc, "s_cc")
    s_v = Sem(nc, "s_v")
    s_s = Sem(nc, "s_s")
    s_g = Sem(nc, "s_g")
    s_t = Sem(nc, "s_t")

    dbg_outs = {}

    def dump(name, ap):
        if not dbg:
            return
        shp = list(ap.shape)
        dt = ap.dtype
        d = nc.dram_tensor("dbg_" + name, shp, dt, kind="ExternalOutput").ap()
        dbg_outs[name] = d
        for sem in (s_v, s_s, s_g, s_t):
            b.wait("sync", sem, sem.c)
        b.op("sync", lambda e, o=d, i=ap: e.dma_start(out=o, in_=i, allow_slow_non_contiguous=True),
             inc=s_out, inc_n=16)
        for eng in ("vector", "scalar", "gpsimd", "tensor"):
            b.wait(eng, s_out, s_out.c)

    x3 = dx.rearrange("(cc p) t -> p cc t", p=P)
    y3 = dy.rearrange("(cc p) t -> p cc t", p=P)

    for (src, dst, nchunk) in ((dw1, w1b, 2), (dw2, w2b, 2), (dh1, h1b, 8), (dh2, h2b, 8), (dw3, w3b, 4)):
        step = src.shape[0] // nchunk
        for i in range(nchunk):
            b.dma("gpsimd", dst[i * step:(i + 1) * step], src[i * step:(i + 1) * step], s_cast)
    CAST_W12, CAST_H1, CAST_H2, CAST_ALL = 16 * 4, 16 * 12, 16 * 20, 16 * 24

    for n in vec_names:
        b.dma("sync", V[n][:], dvecs[n], s_w)
    for (t, h) in ((IDE, ident_h), (OC, oc_h), (Zc, zero_h), (CB, eps_h), (ONEC, onec_h)):
        b.dma("sync", t[:], h.ap(), s_w)
    b.dma("sync", Msk[:], dmask, s_w)
    b.dma("sync", Xpv[:], dxprev, s_w)
    for (t, d) in ((S1t, ds1), (S2t, ds2), (Sh1t, dsh1), (Sh2t, dsh2), (S3t, ds3)):
        b.dma("sync", t[:], d, s_w)
    b.dma("sync", X0[:], x3[:, :, 0:SL], s_x)
    NVEC = s_w.c

    b.wait("scalar", s_w, NVEC)
    b.op("scalar", lambda e: e.activation(Fd[:], V["td"][:], AF.Exp), inc=s_s)
    b.op("scalar", lambda e: e.activation(Dt[:], Fd[:], AF.Exp, scale=-1.0), inc=s_s)
    b.op("scalar", lambda e: e.activation(C1[:], Fd[:], AF.Exp, scale=-float(SL)), inc=s_s)
    b.op("scalar", lambda e: e.activation(Eu[:], V["tf"][:], AF.Exp), inc=s_s)
    b.op("scalar", lambda e: e.activation(Rd[:], Fd[:], AF.Exp), inc=s_s)
    b.drain("scalar", s_s)
    SD = s_s.c
    b.wait("vector", s_s, SD)
    for cc in range(CC):
        b.op("vector", lambda e, cc=cc: e.tensor_tensor_scan(
            DP[:, cc, :], Dt[:, cc:cc + 1].broadcast_to((P, SL)),
            Zc[:, 0:1].broadcast_to((P, SL)), Rd[:, cc:cc + 1], ALU.mult, ALU.add), inc=s_v)
    b.op("gpsimd", lambda e: e.memset(PiT[:], 0.0), inc=s_g)
    b.op("gpsimd", lambda e: e.memset(QiT[:], 0.0), inc=s_g)
    MEMS = s_g.c

    b.op("scalar", lambda e: e.activation(PRS[:, 0:CC], Xpv[:], AF.Square), inc=s_s)
    SQ0 = s_s.c
    b.wait("vector", s_s, SQ0)
    b.wait("vector", s_w, NVEC)
    b.op("vector", lambda e: e.tensor_reduce(Fd[:, 0:1], Xpv[:], AX, ALU.add), inc=s_v)
    b.op("vector", lambda e: e.tensor_reduce(Fd[:, 1:2], PRS[:, 0:CC], AX, ALU.add), inc=s_v)
    b.drain("vector", s_v)
    V0 = s_v.c
    b.wait("tensor", s_v, V0)
    b.wait("tensor", s_w, NVEC)
    b.op("tensor", lambda e: e.transpose(PT[0:1, 0, 0:P], Fd[:, 0:1], IDE[:]), inc=s_t)
    b.op("tensor", lambda e: e.transpose(PT[0:1, 1, 0:P], Fd[:, 1:2], IDE[:]), inc=s_t)
    T0 = s_t.c
    b.wait("vector", s_t, T0)
    b.op("vector", lambda e: e.tensor_reduce(Sc1[0:1, 0:1], PT[0:1, 0, 0:P], AX, ALU.add), inc=s_v)
    b.op("vector", lambda e: e.tensor_reduce(Sc1[0:1, 1:2], PT[0:1, 1, 0:P], AX, ALU.add), inc=s_v)
    b.drain("vector")
    b.op("vector", lambda e: e.tensor_scalar_mul(Sc1[0:1, 0:1], Sc1[0:1, 0:1], 1.0 / 1024.0), inc=s_v)
    b.drain("vector")
    b.op("vector", lambda e: e.tensor_mul(Sc1[0:1, 2:3], Sc1[0:1, 0:1], Sc1[0:1, 0:1]), inc=s_v)
    b.op("vector", lambda e: e.tensor_scalar_mul(Sc1[0:1, 1:2], Sc1[0:1, 1:2], 1.0 / 1024.0), inc=s_v)
    b.drain("vector")
    b.op("vector", lambda e: e.tensor_sub(Sc1[0:1, 1:2], Sc1[0:1, 1:2], Sc1[0:1, 2:3]), inc=s_v)
    b.drain("vector", s_v)
    V1 = s_v.c
    b.wait("scalar", s_v, V1)
    b.op("scalar", lambda e: e.activation(Sc1[0:1, 1:2], Sc1[0:1, 1:2], AF.Ln, bias=CB[0:1, 0:1]), inc=s_s)
    b.drain("scalar")
    b.op("scalar", lambda e: e.activation(Sc1[0:1, 1:2], Sc1[0:1, 1:2], AF.Exp, scale=-0.5), inc=s_s)
    b.drain("scalar", s_s)
    SRS = s_s.c
    b.wait("tensor", s_s, SRS)
    b.op("tensor", lambda e: e.matmul(PB0[:, 0:1], OC[0:1, 0:P], Sc1[0:1, 0:1], start=True, stop=True), inc=s_t)
    b.op("tensor", lambda e: e.matmul(PB1[:, 0:1], OC[0:1, 0:P], Sc1[0:1, 1:2], start=True, stop=True), inc=s_t)
    T1 = s_t.c
    b.wait("vector", s_t, T1)
    b.op("vector", lambda e: e.tensor_sub(PRS[:, CC:2 * CC], Xpv[:], PB0[:, 0:1].broadcast_to((P, CC))), inc=s_v)
    b.drain("vector")
    b.op("vector", lambda e: e.tensor_mul(H[:, :, 0], PRS[:, CC:2 * CC], PB1[:, 0:1].broadcast_to((P, CC))), inc=s_v)
    b.drain("vector", s_v)
    VH0 = s_v.c

    def prep_attn(dsrc, dest_sb, isw_t, store_mmaj=None, dbg_pfx=None):
        b.wait("sync", s_v, s_v.c)
        b.wait("sync", s_g, s_g.c)
        for k in range(CC):
            b.dma("sync", PRW[:, k, :], dsrc[k].rearrange("p m f -> p (m f)"), s_w)
        b.wait("vector", s_w, s_w.c)
        if dbg_pfx:
            b.wait("sync", s_w, s_w.c)
            dump(dbg_pfx + "_prw0", PRW[:, 0, :])
        b.op("vector", lambda e: e.tensor_reduce(Fd[:, 0:1], PRW[:].rearrange("p k mf -> p (k mf)"),
                                                 AX, ALU.add, apply_absolute_value=True), inc=s_v)
        VA = s_v.c
        b.wait("tensor", s_v, VA)
        b.op("tensor", lambda e: e.transpose(PT[0:1, 0, 0:P], Fd[:, 0:1], IDE[:]), inc=s_t)
        TA = s_t.c
        b.wait("vector", s_t, TA)
        b.op("vector", lambda e: e.tensor_reduce(Sc1[0:1, 0:1], PT[0:1, 0, 0:P], AX, ALU.add), inc=s_v)
        b.drain("vector")
        b.op("vector", lambda e: e.tensor_scalar(Sc1[0:1, 0:1], Sc1[0:1, 0:1], 1.0 / (1024.0 * 1024.0), 1e-5,
                                                 ALU.mult, ALU.max), inc=s_v)
        b.drain("vector")
        b.op("vector", lambda e: e.tensor_scalar_mul(Sc1[0:1, 2:3], Sc1[0:1, 0:1], 1.0 / 127.0), inc=s_v)
        b.drain("vector", s_v)
        b.op("vector", lambda e: e.reciprocal(Sc1[0:1, 1:2], Sc1[0:1, 0:1]), inc=s_v)
        b.drain("vector", s_v)
        VB0 = s_v.c
        b.wait("tensor", s_v, VB0)
        b.op("tensor", lambda e: e.matmul(PB0[:, 0:1], OC[0:1, 0:P], Sc1[0:1, 1:2], start=True, stop=True), inc=s_t)
        b.op("tensor", lambda e: e.matmul(PB1[:, 0:1], OC[0:1, 0:P], Sc1[0:1, 2:3], start=True, stop=True), inc=s_t)
        TB = s_t.c
        b.wait("vector", s_t, TB)
        b.op("vector", lambda e: e.tensor_copy(SWB[:], PB0[:, 0:1]), inc=s_v)
        b.op("vector", lambda e, t=isw_t: e.tensor_copy(t[:], PB1[:, 0:1]), inc=s_v)
        b.drain("vector", s_v)
        if dbg_pfx:
            dump(dbg_pfx + "_swb", SWB[:])
        for k in range(CC):
            b.op("vector", lambda e, k=k: e.tensor_scalar_mul(PRS[:], PRW[:, k, :], SWB[:, 0:1]), inc=s_v)
            b.drain("vector")
            b.op("vector", lambda e: e.tensor_scalar(PRS[:], PRS[:], MAGIC, -MAGIC, ALU.add, ALU.add), inc=s_v)
            b.drain("vector", s_v)
            if dest_sb is not None:
                b.op("vector", lambda e, k=k: e.tensor_scalar(dest_sb[:, k, :], PRS[:], 1.0, -1.0,
                                                              ALU.min, ALU.max), inc=s_v)
            else:
                b.op("vector", lambda e: e.tensor_scalar(PRB[:], PRS[:], 1.0, -1.0, ALU.min, ALU.max), inc=s_v)
                VD = s_v.c
                b.wait("scalar", s_v, VD)
                b.op("scalar", lambda e, k=k: e.dma_start(
                    out=store_mmaj[:, :, k, :].rearrange("m p f -> p m f"),
                    in_=PRB[:].rearrange("p (m f) -> p m f", m=CC)), inc=s_pw, inc_n=16)
                b.wait("vector", s_pw, s_pw.c)

    prep_attn(dwk, WQK, ISW["k"], dbg_pfx="wk" if dbg else None)
    prep_attn(dwv, WQV, ISW["v"])
    prep_attn(dwr, WQR, ISW["r"])
    PREP_KVR_V = s_v.c
    prep_attn(dwo, None, ISW["o"], store_mmaj=wqo_h)
    WQO_N = s_pw.c
    dump("Dt", Dt[:]); dump("Eu", Eu[:]); dump("C1", C1[:]); dump("Rd", Rd[:])
    dump("iswk", ISW["k"][:]); dump("iswo", ISW["o"][:])
    dump("WQK", WQK[:]); dump("h0", H[:, :, 0])
    dump("DPs", DP[:, :, 0:8])

    def mmsum(lhsT_fn, bank, dst):
        for blk in range(4):
            for cc in range(CC):
                b.op("tensor", lambda e, cc=cc, blk=blk: e.matmul(
                    PT[:, bank, blk:blk + 1], lhsT_fn(cc, blk), ONEC[:],
                    start=(cc == 0), stop=(cc == CC - 1)),
                    inc=(s_t if (blk == 3 and cc == CC - 1) else None))
        TS = s_t.c
        b.wait("vector", s_t, TS)
        b.op("vector", lambda e, d=dst: e.tensor_copy(d[:], PT[:, bank, 0:4]), inc=s_v)

    def tokpipe(srcU0, srcU1, red0, tinymath, lhs0, lhs1, mm0=None, mm1=None):
        if mm0 is not None:
            mmsum(mm0, 0, Tok0)
        if mm1 is not None:
            mmsum(mm1, 1, Tok1)
        if mm0 is None or mm1 is None:
            srcU0()
            b.drain("vector", s_v)
            VU = s_v.c
            b.wait("tensor", s_v, VU)
            for i in range(4):
                b.op("tensor", lambda e, i=i: e.transpose(PT[:, 0, i * P:(i + 1) * P],
                                                          U0[:, i * P:(i + 1) * P], IDE[:]),
                     inc=(s_t if i == 3 else None))
            TT = s_t.c
            b.wait("vector", s_t, TT)
            for i in range(4):
                b.op("vector", lambda e, i=i: e.tensor_reduce(Tok0[:, i:i + 1], PT[:, 0, i * P:(i + 1) * P],
                                                              AX, red0), inc=(s_v if i == 3 else None))
        b.drain("vector")
        tinymath()
        b.drain("vector", s_v)
        b.drain("scalar", s_s)
        b.wait("tensor", s_v, s_v.c)
        b.wait("tensor", s_s, s_s.c)
        for i in range(CC):
            b.op("tensor", lambda e, i=i: e.transpose(PT[0:1, i // 4, (i % 4) * P:(i % 4 + 1) * P],
                                                      Fd[:, i:i + 1], IDE[:]),
                 inc=(s_t if i == CC - 1 else None))
        TF = s_t.c
        b.wait("vector", s_t, TF)
        b.op("vector", lambda e: e.tensor_copy(FROW[:], PT[0:1, :, :].rearrange("p a t -> p (a t)")), inc=s_v)
        b.drain("vector", s_v)
        VG = s_v.c
        b.wait("tensor", s_v, VG)
        for i in range(4):
            b.op("tensor", lambda e, i=i: e.matmul(PB0[:, i * P:(i + 1) * P], lhs0, FROW[0:1, i * P:(i + 1) * P],
                                                   start=True, stop=True), inc=(s_t if i == 3 else None))
        for i in range(4):
            b.op("tensor", lambda e, i=i: e.matmul(PB1[:, i * P:(i + 1) * P], lhs1, FROW[0:1, (4 + i) * P:(5 + i) * P],
                                                   start=True, stop=True), inc=(s_t if i == 3 else None))
        b.wait("vector", s_t, s_t.c)

    def ln_math():
        b.op("vector", lambda e: e.tensor_scalar_mul(Fd[:, 0:4], Tok0[:], 1.0 / 1024.0), inc=s_v)
        b.drain("vector")
        b.op("vector", lambda e: e.tensor_mul(Tok0[:], Fd[:, 0:4], Fd[:, 0:4]), inc=s_v)
        b.op("vector", lambda e: e.tensor_scalar_mul(Tok1[:], Tok1[:], 1.0 / 1024.0), inc=s_v)
        b.drain("vector")
        b.op("vector", lambda e: e.tensor_sub(Tok1[:], Tok1[:], Tok0[:]), inc=s_v)
        b.drain("vector", s_v)
        VL = s_v.c
        b.wait("scalar", s_v, VL)
        b.op("scalar", lambda e: e.activation(Tok1[:], Tok1[:], AF.Ln, bias=CB[:, 0:1]), inc=s_s)
        b.drain("scalar")
        b.op("scalar", lambda e: e.activation(Fd[:, 4:8], Tok1[:], AF.Exp, scale=-0.5), inc=s_s)

    def quant_math(isw_t):
        b.op("vector", lambda e: e.tensor_scalar_max(Tok0[:], Tok0[:], 4e-5), inc=s_v)
        VQ = s_v.c
        b.wait("scalar", s_v, VQ)
        b.op("scalar", lambda e: e.activation(Fd[:, 0:4], Tok0[:], AF.Ln), inc=s_s)
        b.op("scalar", lambda e: e.activation(Fd[:, 0:4], Fd[:, 0:4], AF.Exp, scale=-1.0), inc=s_s)
        b.op("scalar", lambda e: e.activation(Tok1[:], Tok1[:], AF.Ln, scale=1.0 / 1024.0, bias=CB[:, 1:2]), inc=s_s)
        b.op("scalar", lambda e: e.activation(Tok1[:], Tok1[:], AF.Exp, scale=-0.5), inc=s_s)
        SQm = s_s.c
        b.wait("vector", s_s, SQm)
        b.op("vector", lambda e: e.tensor_mul(Fd[:, 4:8], Tok0[:], Tok1[:]), inc=s_v)
        b.op("vector", lambda e, t=isw_t: e.tensor_scalar_mul(Fd[:, 4:8], Fd[:, 4:8], t[:, 0:1]), inc=s_v)

    def emit_ln(x_t, out_h, sq_t, c_t):
        b.wait("scalar", s_v, s_v.c)
        b.wait("tensor", s_v, s_v.c)
        b.op("scalar", lambda e, x=x_t, s=sq_t: e.activation(s[:], x[:], AF.Square), inc=s_s)
        b.drain("scalar", s_s)
        SS = s_s.c
        b.wait("tensor", s_s, SS)
        tokpipe(None, None, ALU.add, ln_math, OC[0:1, 0:P], OC[0:1, 0:P],
                mm0=lambda cc, blk, x=x_t: x[:, cc, blk * P:(blk + 1) * P],
                mm1=lambda cc, blk, s=sq_t: s[:, cc, blk * P:(blk + 1) * P])
        b.op("vector", lambda e, x=x_t, c=c_t: e.tensor_sub(
            c[:], x[:], PB0[:].unsqueeze(1).broadcast_to((P, CC, SL))), inc=s_v)
        b.op("vector", lambda e, c=c_t, o=out_h: e.tensor_mul(
            o, c[:], PB1[:].unsqueeze(1).broadcast_to((P, CC, SL))), inc=s_v)
        b.wait("tensor", s_v, s_v.c)

    def quant_head(kin_t, rms_t, y2_t, sq_t):
        b.wait("scalar", s_v, s_v.c)
        b.wait("scalar", s_t, s_t.c)
        for cc in range(CC):
            b.op("scalar", lambda e, cc=cc, k=kin_t, y=y2_t, r=rms_t: e.activation(
                y[:, cc, :], k[:, cc, :], AF.Copy, scale=r[:, cc:cc + 1]), inc=(s_s if cc == CC - 1 else None))
        b.op("scalar", lambda e, k=kin_t, s=sq_t: e.activation(s[:], k[:], AF.Square), inc=s_s)
        b.drain("scalar", s_s)
        return s_s.c

    def quant_tail(SY, isw_t, y2_t, sq_t, z_t):
        b.wait("tensor", s_s, SY)

        def u0():
            b.wait("vector", s_s, SY)
            b.op("vector", lambda e, y=y2_t: e.tensor_reduce(U0[:], y[:].transpose([0, 2, 1]), AX, ALU.max,
                                                             apply_absolute_value=True), inc=s_v)

        tokpipe(u0, None, ALU.max, lambda: quant_math(isw_t), OC[0:1, P:2 * P], OC[0:1, 0:P],
                mm1=lambda cc, blk, s=sq_t: s[:, cc, blk * P:(blk + 1) * P])
        b.op("vector", lambda e, y=y2_t: e.tensor_mul(y[:], y[:], PB0[:].unsqueeze(1).broadcast_to((P, CC, SL))),
             inc=s_v)
        b.op("vector", lambda e: e.tensor_copy(DBC[:], PB1[:]), inc=s_v)
        b.op("vector", lambda e, y=y2_t, z=z_t: e.tensor_scalar(z[:], y[:], MAGIC, -MAGIC, ALU.add, ALU.add), inc=s_v)
        b.drain("vector", s_v)
        b.wait("tensor", s_v, s_v.c)

    def emit_quant(kin_t, rms_t, isw_t, y2_t, sq_t, z_t):
        SY = quant_head(kin_t, rms_t, y2_t, sq_t)
        quant_tail(SY, isw_t, y2_t, sq_t, z_t)

    def emit_mm1024(wq_ap_fn, z_t, evac, nK=CC):
        evc = {}
        for m in range(CC):
            if m >= 4:
                b.wait("tensor", s_v, evc[m - 4])
            bank = PK[:, m % 4, :]
            for k in range(nK):
                b.op("tensor", lambda e, m=m, k=k, bank=bank, z=z_t: e.matmul(
                    bank, wq_ap_fn(m, k), z[:, k, :], start=(k == 0), stop=(k == nK - 1)),
                    inc=(s_t if k == nK - 1 else None))
            TM = s_t.c
            b.wait("vector", s_t, TM)
            evac(m, bank)
            evc[m] = s_v.c

    b.wait("tensor", s_v, PREP_KVR_V)
    for s in range(NS):
        b.wait("vector", s_x, s_x.c)
        b.wait("scalar", s_x, s_x.c)
        b.wait("tensor", s_x, s_x.c)
        if s == 0:
            b.wait("vector", s_v, VH0)
        emit_ln(X0, H[:, :, 1:SL + 1], A3, A2)
        if s == 0:
            dump("U0_ln", U0[:]); dump("U1_ln", U1[:])
            dump("Tok0_ln", Tok0[:]); dump("Tok1_ln", Tok1[:])
            dump("Fd_ln", Fd[:]); dump("FROW_ln", FROW[:]); dump("X0_s0", X0[:])
        VLN = s_v.c
        if s + 1 < NS:
            b.wait("sync", s_v, VLN)
            b.dma("sync", X0[:], x3[:, :, (s + 1) * SL:(s + 2) * SL], s_x)
        b.op("vector", lambda e: e.tensor_sub(Zb[:], H[:, :, 1:SL + 1], H[:, :, 0:SL]), inc=s_v)
        VD_ = s_v.c

        projs = (("tmk", "rmsk", "k", WQK, ZK, A0),
                 ("tmv", "rmsv", "v", WQV, ZV, A1),
                 ("tmr", "rmsr", "r", WQR, ZR, A4))

        def emit_kin(pi):
            tm = projs[pi][0]
            b.wait("vector", s_s, s_s.c)
            for cc in range(CC):
                b.op("vector", lambda e, cc=cc, tm=tm: e.scalar_tensor_tensor(
                    A2[:, cc, :], Zb[:, cc, :], V[tm][:, cc:cc + 1], H[:, cc, 0:SL], ALU.mult, ALU.add),
                    inc=(s_v if cc == CC - 1 else None))
            b.drain("vector", s_v)

        emit_kin(0)
        SYs = {0: quant_head(A2, V[projs[0][1]], A4, A3)}
        for pi in range(3):
            tm, rms, isw, wq, z_t, kdst = projs[pi]
            if s > 0:
                b.wait("vector", s_sp, 16 * 3 * s)
                if pi == 2:
                    b.wait("scalar", s_sp, 16 * 3 * s)
            quant_tail(SYs[pi], ISW[isw], A4, A3, z_t)
            if pi < 2:
                emit_kin(pi + 1)
                SYs[pi + 1] = quant_head(A2, V[projs[pi + 1][1]], A4, A3)

            def evac(m, bank, kdst=kdst, pi=pi):
                if pi != 2:
                    b.op("vector", lambda e, m=m, bank=bank, kd=kdst: e.tensor_mul(
                        kd[:, m, 0:SL], bank, DBC[:]), inc=s_v)
                else:
                    b.op("vector", lambda e, m=m, bank=bank, kd=kdst: e.tensor_mul(
                        kd[:, m, :], bank, DBC[:]), inc=s_v)
            emit_mm1024(lambda m, k, wq=wq: wq[:, k, m * P:(m + 1) * P], z_t, evac)

        b.wait("scalar", s_v, s_v.c)
        b.op("scalar", lambda e: e.activation(F2[:], A4[:], AF.Sigmoid), inc=s_s)
        b.op("scalar", lambda e: e.activation(A3[:], A0[:, :, 0:SL], AF.Exp), inc=s_s)
        SEK = s_s.c
        b.wait("vector", s_s, SEK)
        b.op("vector", lambda e: e.tensor_mul(A4[:], A3[:], A1[:, :, 0:SL]), inc=s_v)
        if s == 0:
            b.wait("vector", s_g, MEMS)
        b.op("vector", lambda e: e.tensor_copy(A0[:, :, 0], PiT[:]), inc=s_v)
        b.op("vector", lambda e: e.tensor_copy(A1[:, :, 0], QiT[:]), inc=s_v)
        for cc in range(CC):
            b.op("vector", lambda e, cc=cc: e.tensor_tensor_scan(
                A0[:, cc, 1:SL + 1], Dt[:, cc:cc + 1].broadcast_to((P, SL)), A4[:, cc, :],
                PiT[:, cc:cc + 1], ALU.mult, ALU.add), inc=(s_v if cc == CC - 1 else None))
        for cc in range(CC):
            b.op("vector", lambda e, cc=cc: e.tensor_tensor_scan(
                A1[:, cc, 1:SL + 1], Dt[:, cc:cc + 1].broadcast_to((P, SL)), A3[:, cc, :],
                QiT[:, cc:cc + 1], ALU.mult, ALU.add), inc=(s_v if cc == CC - 1 else None))
        for cc in range(CC):
            b.op("vector", lambda e, cc=cc: e.scalar_tensor_tensor(
                F0[:, cc, :], A4[:, cc, :], Eu[:, cc:cc + 1], A0[:, cc, 0:SL], ALU.mult, ALU.add),
                inc=(s_v if cc == CC - 1 else None))
        for cc in range(CC):
            b.op("vector", lambda e, cc=cc: e.scalar_tensor_tensor(
                F1[:, cc, :], A3[:, cc, :], Eu[:, cc:cc + 1], A1[:, cc, 0:SL], ALU.mult, ALU.add),
                inc=(s_v if cc == CC - 1 else None))
        b.op("vector", lambda e: e.tensor_copy(PiT[:], A0[:, :, SL]), inc=s_v)
        b.op("vector", lambda e: e.tensor_copy(QiT[:], A1[:, :, SL]), inc=s_v)
        b.drain("vector", s_v)
        b.wait("vector", s_g, s_g.c)
        b.op("vector", lambda e: e.tensor_copy(H[:, :, 0], H[:, :, SL]), inc=s_v)
        VSP = s_v.c
        if s == 0:
            dump("H_s0", H[:]); dump("pst_s0", A0[:]); dump("qst_s0", A1[:])
            dump("ek_s0", A3[:]); dump("EV_s0", A4[:])
            dump("F0_s0", F0[:]); dump("F1_s0", F1[:]); dump("F2_s0", F2[:])
        b.wait("scalar", s_v, VSP)
        b.op("scalar", lambda e, s=s: e.dma_start(out=sp_n[s], in_=F0[:].rearrange("p c t -> p (c t)")),
             inc=s_sp, inc_n=16)
        b.op("scalar", lambda e, s=s: e.dma_start(out=sp_d[s], in_=F1[:].rearrange("p c t -> p (c t)")),
             inc=s_sp, inc_n=16)
        b.op("scalar", lambda e, s=s: e.dma_start(out=sp_r[s], in_=F2[:].rearrange("p c t -> p (c t)")),
             inc=s_sp, inc_n=16)

    b.drain("vector")
    b.op("vector", lambda e: e.tensor_copy(Gio[:, 0:8], PiT[:]), inc=s_v)
    b.op("vector", lambda e: e.tensor_copy(Gio[:, 8:16], QiT[:]), inc=s_v)
    b.drain("vector", s_v)
    VCC = s_v.c
    b.wait("gpsimd", s_v, VCC)
    b.dma("gpsimd", cc_in, Gio[:], s_cc)
    b.wait("gpsimd", s_cc, 16)
    b.op("gpsimd", lambda e: e.collective_compute(
        "AllGather", ALU.bypass, replica_groups=[[0, 1], [2, 3], [4, 5], [6, 7]],
        ins=[cc_in], outs=[cc_out]), inc=s_cc)
    b.wait("gpsimd", s_cc, 17)
    b.dma("gpsimd", Gin[:], cc_out[0:P, :], s_cc)
    b.wait("vector", s_cc, 33)
    b.op("vector", lambda e: e.tensor_mul(Pcs[:], Gin[:, 0:8], Msk[:]), inc=s_v)
    b.op("vector", lambda e: e.tensor_mul(Qcs[:], Gin[:, 8:16], Msk[:]), inc=s_v)
    b.drain("vector", s_v)
    VCM = s_v.c

    b.wait("scalar", s_v, VCM)
    b.wait("scalar", s_sp, 16 * 3 * NS)
    b.op("scalar", lambda e: e.dma_start(out=F0[:].rearrange("p c t -> p (c t)"), in_=sp_n[0]), inc=s_rl, inc_n=16)
    b.op("scalar", lambda e: e.dma_start(out=F1[:].rearrange("p c t -> p (c t)"), in_=sp_d[0]), inc=s_rl, inc_n=16)
    b.op("scalar", lambda e: e.dma_start(out=F2[:].rearrange("p c t -> p (c t)"), in_=sp_r[0]), inc=s_rl, inc_n=16)
    b.wait("sync", s_v, VCM)
    b.dma("sync", X0[:], x3[:, :, 0:SL], s_x)

    free_at = {}

    def stream_load(eng, buf, src_ap, sem):
        key = buf.name
        if key in free_at:
            sem2, cnt = free_at[key]
            b.wait(eng, sem2, cnt)
        b.dma(eng, buf[:].rearrange("p k f -> p (k f)"), src_ap, sem)

    for s in range(NS):
        b.wait("vector", s_rl, 48 * (s + 1))
        b.wait("vector", s_x, s_x.c)
        if s > 0:
            b.op("vector", lambda e: e.tensor_mul(Pcs[:], Pcs[:], C1[:]), inc=s_v)
            b.op("vector", lambda e: e.tensor_mul(Qcs[:], Qcs[:], C1[:]), inc=s_v)
            b.drain("vector", s_v)
        for cc in range(CC):
            b.op("vector", lambda e, cc=cc: e.scalar_tensor_tensor(
                F0[:, cc, :], DP[:, cc, :], Pcs[:, cc:cc + 1], F0[:, cc, :], ALU.mult, ALU.add),
                inc=(s_v if cc == CC - 1 else None))
        for cc in range(CC):
            b.op("vector", lambda e, cc=cc: e.scalar_tensor_tensor(
                F1[:, cc, :], DP[:, cc, :], Qcs[:, cc:cc + 1], F1[:, cc, :], ALU.mult, ALU.add),
                inc=(s_v if cc == CC - 1 else None))
        VCR = s_v.c
        b.wait("scalar", s_v, VCR)
        b.op("scalar", lambda e: e.activation(QS[:], F1[:], AF.Ln), inc=s_s)
        b.op("scalar", lambda e: e.activation(QS[:], QS[:], AF.Exp, scale=-1.0), inc=s_s)
        SIV = s_s.c
        b.wait("vector", s_s, SIV)
        if s == 0:
            dump("nc_s0", F0[:]); dump("dc_s0", F1[:]); dump("inv_s0", QS[:])
        b.op("vector", lambda e: e.tensor_mul(YF[:], F0[:], QS[:]), inc=s_v)
        b.op("vector", lambda e: e.tensor_mul(YF[:], YF[:], F2[:]), inc=s_v)
        VY = s_v.c
        if s == 0:
            dump("y_s0", YF[:])
        emit_quant(YF, V["rmso"], ISW["o"], QS, SQB2, ZK)
        if s == 0:
            b.wait("sync", s_pw, WQO_N)

        def evac_o(m, bank):
            b.op("vector", lambda e, m=m, bank=bank: e.tensor_mul(QS[:, m, :], bank, DBC[:]), inc=s_v)

        def wqo_ap(m, k):
            return (FW0 if m % 2 == 0 else FW1)[:, k, :]

        for m in range(CC):
            buf = FW0 if m % 2 == 0 else FW1
            stream_load("sync", buf, wqo_h[m].rearrange("p k f -> p (k f)"), s_fh)
            LW = s_fh.c
            if m >= 4:
                pass
            b.wait("tensor", s_fh, LW)
            bank = PK[:, m % 4, :]
            for k in range(CC):
                b.op("tensor", lambda e, k=k, bank=bank, buf=buf: e.matmul(
                    bank, buf[:, k, :], ZK[:, k, :], start=(k == 0), stop=(k == CC - 1)),
                    inc=(s_t if k == CC - 1 else None))
            free_at[buf.name] = (s_t, s_t.c)
            b.wait("vector", s_t, s_t.c)
            evac_o(m, bank)
        if s + 1 < NS:
            b.wait("scalar", s_t, s_t.c)
            b.wait("scalar", s_v, s_v.c)
            b.op("scalar", lambda e, s=s: e.dma_start(out=F0[:].rearrange("p c t -> p (c t)"), in_=sp_n[s + 1]),
                 inc=s_rl, inc_n=16)
            b.op("scalar", lambda e, s=s: e.dma_start(out=F1[:].rearrange("p c t -> p (c t)"), in_=sp_d[s + 1]),
                 inc=s_rl, inc_n=16)
            b.op("scalar", lambda e, s=s: e.dma_start(out=F2[:].rearrange("p c t -> p (c t)"), in_=sp_r[s + 1]),
                 inc=s_rl, inc_n=16)
        b.op("vector", lambda e: e.tensor_add(X0[:], X0[:], QS[:]), inc=s_v)
        if s == 0:
            dump("xp_s0", X0[:])
        emit_ln(X0, H[:, :, 0:SL], QS, YF)
        VG2 = s_v.c
        if s == 0:
            dump("g_s0", H[:, :, 0:SL])

        bank_free = {}
        svlast = 0
        for m in range(MH):
            stream_load("scalar", FW2 if m % 2 == 0 else FW3, w1b[m].rearrange("p k f -> p (k f)"), s_fw)
            stream_load("scalar", FW0 if m % 2 == 0 else FW1, w2b[m].rearrange("p k f -> p (k f)"), s_fw)
            LW = s_fw.c
            if m == 0:
                b.wait("tensor", s_v, VG2)
                b.wait("tensor", s_cast, CAST_W12)
            b.wait("tensor", s_fw, LW)
            w1t = FW2 if m % 2 == 0 else FW3
            w2t = FW0 if m % 2 == 0 else FW1
            ia, ib = (2 * m) % 4, (2 * m + 1) % 4
            for i in (ia, ib):
                if i in bank_free:
                    sem2, cnt = bank_free[i]
                    b.wait("tensor", sem2, cnt)
            bank_a, bank_b = PK[:, ia, :], PK[:, ib, :]
            for k in range(CC):
                b.op("tensor", lambda e, k=k, bank=bank_a, w=w1t: e.matmul(
                    bank, w[:, k, :], H[:, k, 0:SL], start=(k == 0), stop=(k == CC - 1)),
                    inc=(s_t if k == CC - 1 else None))
            TA1 = s_t.c
            for k in range(CC):
                b.op("tensor", lambda e, k=k, bank=bank_b, w=w2t: e.matmul(
                    bank, w[:, k, :], H[:, k, 0:SL], start=(k == 0), stop=(k == CC - 1)),
                    inc=(s_t if k == CC - 1 else None))
            TA2 = s_t.c
            free_at[w1t.name] = (s_t, TA1)
            free_at[w2t.name] = (s_t, TA2)
            b.wait("scalar", s_t, TA1)
            b.wait("scalar", s_v, svlast)
            b.op("scalar", lambda e, m=m, bank=bank_a: e.activation(
                ABLK[:], bank, AF.Silu, scale=S1t[:, m:m + 1]), inc=s_s)
            SA = s_s.c
            bank_free[ia] = (s_s, SA)
            b.wait("vector", s_t, TA2)
            b.wait("vector", s_s, SA)
            b.op("vector", lambda e, m=m, bank=bank_b: e.scalar_tensor_tensor(
                HB[:, m, :], bank, S2t[:, m:m + 1], ABLK[:], ALU.mult, ALU.mult), inc=s_v)
            svlast = s_v.c
            bank_free[ib] = (s_v, svlast)

        def ffn_big(src_t, dst_t, wsrc, scl, cast_need, nM=MH):
            nonlocal bank_free
            b.wait("sync", s_cast, cast_need)
            for m in range(nM):
                buf = FH0 if m % 2 == 0 else FH1
                stream_load("sync", buf, wsrc[m].rearrange("p k f -> p (k f)"), s_fh)
                LH = s_fh.c
                b.wait("tensor", s_fh, LH)
                i = m % 4
                if i in bank_free:
                    sem2, cnt = bank_free[i]
                    b.wait("tensor", sem2, cnt)
                bank = PK[:, i, :]
                for k in range(MH):
                    b.op("tensor", lambda e, k=k, bank=bank, buf=buf, src=src_t: e.matmul(
                        bank, buf[:, k, :], src[:, k, :], start=(k == 0), stop=(k == MH - 1)),
                        inc=(s_t if k == MH - 1 else None))
                TH = s_t.c
                free_at[buf.name] = (s_t, TH)
                b.wait("scalar", s_t, TH)
                if dst_t is QS:
                    b.op("scalar", lambda e, m=m, bank=bank: e.activation(
                        QS[:, m, :], bank, AF.Copy, scale=scl[:, m:m + 1]), inc=s_s)
                else:
                    b.op("scalar", lambda e, m=m, bank=bank, dst=dst_t, scl=scl: e.activation(
                        dst[:, m, :], bank, AF.Gelu, scale=scl[:, m:m + 1]), inc=s_s)
                bank_free[i] = (s_s, s_s.c)

        ffn_big(HB, H2B, h1b, Sh1t, CAST_H1)
        ffn_big(H2B, HB, h2b, Sh2t, CAST_H2)
        ffn_big(HB, QS, w3b, S3t, CAST_ALL, nM=CC)
        SW3 = s_s.c
        b.wait("vector", s_s, SW3)
        b.op("vector", lambda e: e.tensor_add(QS[:], QS[:], X0[:]), inc=s_v)
        VOUT = s_v.c
        b.wait("gpsimd", s_v, VOUT)
        b.op("gpsimd", lambda e, s=s: e.dma_start(out=y3[:, :, s * SL:(s + 1) * SL], in_=QS[:]),
             inc=s_out, inc_n=16)
        OUTC = s_out.c
        if s + 1 < NS:
            b.wait("sync", s_v, VOUT)
            b.dma("sync", X0[:], x3[:, :, (s + 1) * SL:(s + 2) * SL], s_x)
        b.wait("scalar", s_out, OUTC)

    b.wait("gpsimd", s_out, s_out.c)
    b.wait("gpsimd", s_sp, 16 * 3 * NS)

    blk = nc.Block()
    blk.__enter__()
    for eng in B.ENGS:
        getattr(blk, eng)(lambda e, eng=eng: [f(e) for f in b.ops[eng]])
    blk.__exit__(None, None, None)
    return nc


def _host_prepare(inputs):
    x = np.ascontiguousarray(np.asarray(inputs["L0_attn_hidden"], dtype=np.float32))

    def tile_kpmf(w, KB, MB):
        return np.ascontiguousarray(np.asarray(w, np.float32).reshape(MB, P, KB, P).transpose(2, 3, 0, 1))

    def tile_mpkf(w, MB, KB):
        return np.ascontiguousarray(np.asarray(w, np.float32).reshape(MB, P, KB, P).transpose(0, 3, 2, 1))

    def vec_pc(v):
        return np.ascontiguousarray(np.asarray(v, np.float32).reshape(-1)[:1024].reshape(CC, P).T)

    def vec_pm(v, MB):
        return np.ascontiguousarray(np.asarray(v, np.float32).reshape(MB, P).T)

    common = dict(
        wk=tile_kpmf(inputs["L0_attn_w_key"], CC, CC),
        wv=tile_kpmf(inputs["L0_attn_w_value"], CC, CC),
        wr=tile_kpmf(inputs["L0_attn_w_receptance"], CC, CC),
        wo=tile_kpmf(inputs["L0_attn_w_output"], CC, CC),
        w1=tile_mpkf(inputs["L0_ffn_w1_w"], MH, CC),
        w2=tile_mpkf(inputs["L0_ffn_w2_w"], MH, CC),
        h1=tile_mpkf(inputs["L0_ffn_h1_w"], MH, MH),
        h2=tile_mpkf(inputs["L0_ffn_h2_w"], MH, MH),
        w3=tile_mpkf(inputs["L0_ffn_w3_w"], CC, MH),
        tmk=vec_pc(inputs["L0_attn_time_mix_key"]),
        tmv=vec_pc(inputs["L0_attn_time_mix_value"]),
        tmr=vec_pc(inputs["L0_attn_time_mix_receptance"]),
        rmsk=vec_pc(inputs["L0_attn_rms_key"]),
        rmsv=vec_pc(inputs["L0_attn_rms_value"]),
        rmsr=vec_pc(inputs["L0_attn_rms_receptance"]),
        rmso=vec_pc(inputs["L0_attn_rms_output"]),
        td=vec_pc(inputs["L0_attn_time_decay"]),
        tf=vec_pc(inputs["L0_attn_time_first"]),
        s1=vec_pm(inputs["L0_ffn_w1_s"], MH),
        s2=vec_pm(inputs["L0_ffn_w2_s"], MH),
        sh1=vec_pm(inputs["L0_ffn_h1_s"], MH),
        sh2=vec_pm(inputs["L0_ffn_h2_s"], MH),
        s3=vec_pm(inputs["L0_ffn_w3_s"], CC),
    )
    in_maps = []
    for c in range(8):
        bb, half = c // 2, c % 2
        xT = np.ascontiguousarray(x[bb, half * TOK:(half + 1) * TOK, :].T)
        if half == 0:
            xprev = np.zeros((P, CC), np.float32)
            mask = np.zeros((P, CC), np.float32)
        else:
            xprev = vec_pc(x[bb, TOK - 1, :])
            mask = np.ones((P, CC), np.float32)
        im = dict(common)
        im["x"] = xT
        im["xprev"] = xprev
        im["mask"] = mask
        in_maps.append(im)
    return in_maps


_NC_CACHE = {}


def kernel(**inputs):
    in_maps = _host_prepare(inputs)
    if "nc" not in _NC_CACHE:
        _NC_CACHE["nc"] = build_nc()
    nc = _NC_CACHE["nc"]
    res = run_bass_kernel_spmd(nc, in_maps, core_ids=list(range(8)))
    out = np.empty((4, 8192, 1024), np.float32)
    for c in range(8):
        bb, half = c // 2, c % 2
        out[bb, half * TOK:(half + 1) * TOK, :] = res.results[c]["y"].T
    return out


# version 15 (final)
# speedup vs baseline: 1.0891x; 1.0891x over previous
import numpy as np
import concourse.bass as bass
import concourse.mybir as mybir
from concourse.bass_utils import run_bass_kernel_spmd

F32 = mybir.dt.float32
BF16 = mybir.dt.bfloat16
F16 = mybir.dt.float16
ALU = mybir.AluOpType
AF = mybir.ActivationFunctionType
AX = mybir.AxisListType.X

P = 128
CC = 8
SL = 512
NS = 8
TOK = NS * SL
MH = 32
MAGIC = float(np.float32(12582912.0))


class Sem:
    def __init__(self, nc, name):
        self.h = nc.alloc_semaphore(name)
        self.c = 0


class B:
    ENGS = ("sync", "vector", "scalar", "tensor", "gpsimd")

    def __init__(self, nc):
        self.nc = nc
        self.ops = {e: [] for e in self.ENGS}

    def wait(self, eng, sem, val=None):
        v = sem.c if val is None else val
        assert v <= sem.c, f"wait on future value {v} > {sem.c}"
        self.ops[eng].append(lambda e, s=sem, v=v: e.wait_ge(s.h, v))

    def op(self, eng, fn, inc=None, inc_n=1):
        if inc is not None:
            inc.c += inc_n
        def run(e, fn=fn, inc=inc, inc_n=inc_n):
            ins = fn(e)
            if inc is not None:
                ins.then_inc(inc.h, inc_n)
        self.ops[eng].append(run)

    def drain(self, eng, sem=None):
        self.op(eng, lambda e: e.drain(), inc=sem)

    def dma(self, eng, out, in_, sem):
        self.op(eng, lambda e, o=out, i=in_: e.dma_start(out=o, in_=i), inc=sem, inc_n=16)


def build_nc(dbg=False):
    nc = bass.Bass("TRN2", target_bir_lowering=False)

    dx = nc.dram_tensor("x", [NS, P, CC, SL], F32, kind="ExternalInput").ap()
    dvp = nc.dram_tensor("vpack", [P, 320], F32, kind="ExternalInput").ap()
    dwk = nc.dram_tensor("wk", [CC, P, CC, P], F32, kind="ExternalInput").ap()
    dwv = nc.dram_tensor("wv", [CC, P, CC, P], F32, kind="ExternalInput").ap()
    dwr = nc.dram_tensor("wr", [CC, P, CC, P], F32, kind="ExternalInput").ap()
    dwo = nc.dram_tensor("wo", [CC, P, CC, P], F32, kind="ExternalInput").ap()
    dw1 = nc.dram_tensor("w1", [MH, P, CC, P], F32, kind="ExternalInput").ap()
    dw2 = nc.dram_tensor("w2", [MH, P, CC, P], F32, kind="ExternalInput").ap()
    dh1 = nc.dram_tensor("h1", [MH, P, MH, P], F32, kind="ExternalInput").ap()
    dh2 = nc.dram_tensor("h2", [MH, P, MH, P], F32, kind="ExternalInput").ap()
    dw3 = nc.dram_tensor("w3", [CC, P, MH, P], F32, kind="ExternalInput").ap()
    vec_names = ["tmk", "tmv", "tmr", "rmsk", "rmsv", "rmsr", "rmso", "td", "tf"]
    dy = nc.dram_tensor("y", [NS, P, CC, SL], F32, kind="ExternalOutput").ap()

    ident_h = nc.inline_tensor(np.eye(P, dtype=np.float32), name="c_ident")
    oc_h = nc.inline_tensor(np.concatenate([np.ones((1, P), np.float32),
                                            np.full((1, P), 127.0, np.float32)], axis=1), name="c_oc")
    zero_h = nc.inline_tensor(np.zeros((P, 1), np.float32), name="c_zero")
    eps_h = nc.inline_tensor(np.tile(np.array([[1e-5, 1e-8]], np.float32), (P, 1)), name="c_eps")
    onec_h = nc.inline_tensor(np.ones((P, 1), np.float32), name="c_onec")
    onec16_h = nc.inline_tensor(np.ones((P, 1), np.float16), name="c_onec16")

    wqo_h = nc.dram_tensor("wqo_s", [CC, P, CC, P], BF16, kind="Internal").ap()
    w1b = nc.dram_tensor("w1b", [MH, P, CC, P], BF16, kind="Internal").ap()
    w2b = nc.dram_tensor("w2b", [MH, P, CC, P], BF16, kind="Internal").ap()
    h1b = nc.dram_tensor("h1b", [MH, P, MH, P], BF16, kind="Internal").ap()
    h2b = nc.dram_tensor("h2b", [MH, P, MH, P], BF16, kind="Internal").ap()
    w3b = nc.dram_tensor("w3b", [CC, P, MH, P], BF16, kind="Internal").ap()
    sp_n = nc.dram_tensor("sp_n", [NS, P, CC * SL], F16, kind="Internal").ap()
    sp_d = nc.dram_tensor("sp_d", [NS, P, CC * SL], F16, kind="Internal").ap()
    sp_r = nc.dram_tensor("sp_r", [NS, P, CC * SL], F16, kind="Internal").ap()
    cc_in = nc.dram_tensor("cc_in", [P, 16], F32, kind="Internal").ap()
    cc_out = nc.dram_tensor("cc_out", [2 * P, 16], F32, kind="Internal").ap()

    BASE = (nc.sbuf_base + 63) // 64 * 64

    def at(name, shape, dtype, off):
        return nc.alloc_sbuf_tensor_at(name, shape, dtype, offset=BASE + off)

    X0 = at("X0", [P, CC, SL], F32, 0)
    Zb = at("Zb", [P, CC, SL], BF16, 16384)
    F0 = at("F0", [P, CC, SL], F16, 24576)
    F1 = at("F1", [P, CC, SL], F16, 32768)
    F2 = at("F2", [P, CC, SL], F16, 40960)
    ZK = at("ZK", [P, CC, SL], BF16, 24576)
    ZV = at("ZV", [P, CC, SL], BF16, 32768)
    ZR = at("ZR", [P, CC, SL], BF16, 40960)
    SQO = at("SQO", [P, CC, SL], F16, 16384)
    Y2O = at("Y2O", [P, CC, SL], F16, 32768)
    H = at("H", [P, CC, SL + 1], BF16, 49152)
    DP = at("DP", [P, CC, SL], F16, 57408)
    RB = 65600
    A0 = at("A0", [P, CC, SL + 1], F32, RB)
    A1 = at("A1", [P, CC, SL + 1], F32, RB + 16416)
    A2 = at("A2", [P, CC, SL], F32, RB + 32832)
    A3 = at("A3", [P, CC, SL], F32, RB + 49216)
    A4 = at("A4", [P, CC, SL], F32, RB + 65600)
    WQK = at("WQK", [P, CC, 1024], BF16, RB + 81984)
    WQV = at("WQV", [P, CC, 1024], BF16, RB + 98368)
    WQR = at("WQR", [P, CC, 1024], BF16, RB + 114752)
    PRW = at("PRW", [P, CC, 1024], F32, RB)
    PRS = at("PRS", [P, 1024], F32, RB + 49216)
    PRB = at("PRB", [P, 1024], BF16, RB + 65600)
    HB = at("HB", [P, MH, SL], BF16, RB)
    H2B = at("H2B", [P, MH, SL], BF16, RB + 32768)
    SQB2 = at("SQB2", [P, CC, SL], F32, RB)
    FH0 = at("FH0", [P, MH, P], BF16, RB + 65536)
    FH1 = at("FH1", [P, MH, P], BF16, RB + 73728)
    FW0 = at("FW0", [P, CC, P], BF16, RB + 81920)
    FW1 = at("FW1", [P, CC, P], BF16, RB + 83968)
    FW2 = at("FW2", [P, CC, P], BF16, RB + 86016)
    FW3 = at("FW3", [P, CC, P], BF16, RB + 88064)
    YF = at("YF", [P, CC, SL], F32, RB + 90112)
    QS = at("QS", [P, CC, SL], F32, RB + 106496)
    FQ0 = at("FQ0", [P, CC, P], BF16, RB + 122880)
    FQ1 = at("FQ1", [P, CC, P], BF16, RB + 124928)
    SM = 196736
    U0 = at("U0", [P, SL], F32, SM)
    U1 = at("U1", [P, SL], F32, SM + 2048)
    DBC = at("DBC", [P, SL], F32, SM + 4096)
    IDE = at("IDE", [P, P], F32, SM + 6144)
    FROW = at("FROW", [1, CC * P], F32, SM + 6656)
    ABLK = at("ABLK", [P, SL], F32, SM + 6656)
    smo = SM + 10752

    def small(name, shape, dtype):
        nonlocal smo
        nbytes = int(np.prod(shape[1:])) * (4 if dtype == F32 else 2)
        t = at(name, shape, dtype, smo)
        smo += (nbytes + 63) // 64 * 64
        return t

    vpk_off = smo
    VPK = small("VPK", [P, 320], F32)
    V = {n: at("v_" + n, [P, CC], F32, vpk_off + 64 * i) for i, n in enumerate(vec_names)}
    Msk = at("Msk", [P, CC], F32, vpk_off + 64 * 9)
    Xpv = at("Xpv", [P, CC], F32, vpk_off + 64 * 10)
    S1t = at("S1t", [P, MH], F32, vpk_off + 64 * 11)
    S2t = at("S2t", [P, MH], F32, vpk_off + 64 * 13)
    Sh1t = at("Sh1t", [P, MH], F32, vpk_off + 64 * 15)
    Sh2t = at("Sh2t", [P, MH], F32, vpk_off + 64 * 17)
    S3t = at("S3t", [P, CC], F32, vpk_off + 64 * 19)
    Dt = small("Dt", [P, CC], F32)
    Rd = small("Rd", [P, CC], F32)
    Eu = small("Eu", [P, CC], F32)
    C1 = small("C1", [P, CC], F32)
    Pin = small("Pin", [P, CC], F32)
    Qin = small("Qin", [P, CC], F32)
    Pcs = small("Pcs", [P, CC], F32)
    Qcs = small("Qcs", [P, CC], F32)
    PiT = small("PiT", [P, CC], F32)
    QiT = small("QiT", [P, CC], F32)
    Gio = small("Gio", [P, 16], F32)
    Gin = small("Gin", [P, 16], F32)
    Tok0 = small("Tok0", [P, 4], F32)
    Tok1 = small("Tok1", [P, 4], F32)
    Fd = small("Fd", [P, 8], F32)
    OC = small("OC", [1, 2 * P], F32)
    Zc = small("Zc", [P, 1], F32)
    ISW = {n: small("isw_" + n, [P, 1], F32) for n in ("k", "v", "r", "o")}
    SWB = small("SWB", [P, 1], F32)
    Sc1 = small("Sc1", [1, 4], F32)
    CB = small("CB", [P, 2], F32)
    ONEC = small("ONEC", [P, 1], F32)
    ONEC16 = small("ONEC16", [P, 1], F16)
    assert BASE + smo <= nc.sbuf_top, (BASE, smo)

    PK = nc.alloc_psum_tensor("PK", [P, 4, SL], F32)
    PB0 = nc.alloc_psum_tensor("PB0", [P, SL], F32)
    PB1 = nc.alloc_psum_tensor("PB1", [P, SL], F32)
    PT = nc.alloc_psum_tensor("PT", [P, 2, SL], F32)

    b = B(nc)
    s_x = Sem(nc, "s_x")
    s_w = Sem(nc, "s_w")
    s_w2 = Sem(nc, "s_w2")
    s_cast = Sem(nc, "s_cast")
    s_fh = Sem(nc, "s_fh")
    s_fw = Sem(nc, "s_fw")
    s_rl = Sem(nc, "s_rl")
    s_pw = Sem(nc, "s_pw")
    s_sp = Sem(nc, "s_sp")
    s_out = Sem(nc, "s_out")
    s_cc = Sem(nc, "s_cc")
    s_v = Sem(nc, "s_v")
    s_s = Sem(nc, "s_s")
    s_g = Sem(nc, "s_g")
    s_t = Sem(nc, "s_t")

    dbg_outs = {}

    def dump(name, ap):
        if not dbg:
            return
        shp = list(ap.shape)
        dt = ap.dtype
        d = nc.dram_tensor("dbg_" + name, shp, dt, kind="ExternalOutput").ap()
        dbg_outs[name] = d
        for sem in (s_v, s_s, s_g, s_t):
            b.wait("sync", sem, sem.c)
        b.op("sync", lambda e, o=d, i=ap: e.dma_start(out=o, in_=i, allow_slow_non_contiguous=True),
             inc=s_out, inc_n=16)
        for eng in ("vector", "scalar", "gpsimd", "tensor"):
            b.wait(eng, s_out, s_out.c)

    b.dma("sync", VPK[:], dvp, s_w)
    for (t, h) in ((IDE, ident_h), (OC, oc_h), (Zc, zero_h), (CB, eps_h), (ONEC, onec_h), (ONEC16, onec16_h)):
        b.dma("gpsimd", t[:], h.ap(), s_w)
    b.dma("sync", X0[:], dx[0], s_x)
    NVEC = s_w.c

    b.wait("scalar", s_w, NVEC)
    b.op("scalar", lambda e: e.activation(Fd[:], V["td"][:], AF.Exp), inc=s_s)
    b.op("scalar", lambda e: e.activation(Dt[:], Fd[:], AF.Exp, scale=-1.0), inc=s_s)
    b.op("scalar", lambda e: e.activation(C1[:], Fd[:], AF.Exp, scale=-float(SL)), inc=s_s)
    b.op("scalar", lambda e: e.activation(Eu[:], V["tf"][:], AF.Exp), inc=s_s)
    b.op("scalar", lambda e: e.activation(Rd[:], Fd[:], AF.Exp), inc=s_s)
    b.drain("scalar", s_s)
    SD = s_s.c
    b.wait("vector", s_s, SD)
    for cc in range(CC):
        b.op("vector", lambda e, cc=cc: e.tensor_tensor_scan(
            DP[:, cc, :], Dt[:, cc:cc + 1].broadcast_to((P, SL)),
            Zc[:, 0:1].broadcast_to((P, SL)), Rd[:, cc:cc + 1], ALU.mult, ALU.add), inc=s_v)
    b.op("gpsimd", lambda e: e.memset(PiT[:], 0.0), inc=s_g)
    b.op("gpsimd", lambda e: e.memset(QiT[:], 0.0), inc=s_g)
    MEMS = s_g.c

    b.op("scalar", lambda e: e.activation(PRS[:, 0:CC], Xpv[:], AF.Square), inc=s_s)
    SQ0 = s_s.c
    b.wait("vector", s_s, SQ0)
    b.wait("vector", s_w, NVEC)
    b.op("vector", lambda e: e.tensor_reduce(Fd[:, 0:1], Xpv[:], AX, ALU.add), inc=s_v)
    b.op("vector", lambda e: e.tensor_reduce(Fd[:, 1:2], PRS[:, 0:CC], AX, ALU.add), inc=s_v)
    b.drain("vector", s_v)
    V0 = s_v.c
    b.wait("tensor", s_v, V0)
    b.wait("tensor", s_w, NVEC)
    b.op("tensor", lambda e: e.transpose(PT[0:1, 0, 0:P], Fd[:, 0:1], IDE[:]), inc=s_t)
    b.op("tensor", lambda e: e.transpose(PT[0:1, 1, 0:P], Fd[:, 1:2], IDE[:]), inc=s_t)
    T0 = s_t.c
    b.wait("vector", s_t, T0)
    b.op("vector", lambda e: e.tensor_reduce(Sc1[0:1, 0:1], PT[0:1, 0, 0:P], AX, ALU.add), inc=s_v)
    b.op("vector", lambda e: e.tensor_reduce(Sc1[0:1, 1:2], PT[0:1, 1, 0:P], AX, ALU.add), inc=s_v)
    b.drain("vector")
    b.op("vector", lambda e: e.tensor_scalar_mul(Sc1[0:1, 0:1], Sc1[0:1, 0:1], 1.0 / 1024.0), inc=s_v)
    b.drain("vector")
    b.op("vector", lambda e: e.tensor_mul(Sc1[0:1, 2:3], Sc1[0:1, 0:1], Sc1[0:1, 0:1]), inc=s_v)
    b.op("vector", lambda e: e.tensor_scalar_mul(Sc1[0:1, 1:2], Sc1[0:1, 1:2], 1.0 / 1024.0), inc=s_v)
    b.drain("vector")
    b.op("vector", lambda e: e.tensor_sub(Sc1[0:1, 1:2], Sc1[0:1, 1:2], Sc1[0:1, 2:3]), inc=s_v)
    b.drain("vector", s_v)
    V1 = s_v.c
    b.wait("scalar", s_v, V1)
    b.op("scalar", lambda e: e.activation(Sc1[0:1, 1:2], Sc1[0:1, 1:2], AF.Ln, bias=CB[0:1, 0:1]), inc=s_s)
    b.drain("scalar")
    b.op("scalar", lambda e: e.activation(Sc1[0:1, 1:2], Sc1[0:1, 1:2], AF.Exp, scale=-0.5), inc=s_s)
    b.drain("scalar", s_s)
    SRS = s_s.c
    b.wait("tensor", s_s, SRS)
    b.op("tensor", lambda e: e.matmul(PB0[:, 0:1], OC[0:1, 0:P], Sc1[0:1, 0:1], start=True, stop=True), inc=s_t)
    b.op("tensor", lambda e: e.matmul(PB1[:, 0:1], OC[0:1, 0:P], Sc1[0:1, 1:2], start=True, stop=True), inc=s_t)
    T1 = s_t.c
    b.wait("vector", s_t, T1)
    b.op("vector", lambda e: e.tensor_sub(PRS[:, CC:2 * CC], Xpv[:], PB0[:, 0:1].broadcast_to((P, CC))), inc=s_v)
    b.drain("vector")
    b.op("vector", lambda e: e.tensor_mul(H[:, :, 0], PRS[:, CC:2 * CC], PB1[:, 0:1].broadcast_to((P, CC))), inc=s_v)
    b.drain("vector", s_v)
    VH0 = s_v.c

    def prep_attn(dsrc, dest_sb, isw_t, store_mmaj=None, dbg_pfx=None):
        b.wait("sync", s_v, s_v.c)
        b.wait("sync", s_g, s_g.c)
        b.wait("scalar", s_v, s_v.c)
        for k in range(CC):
            eng, sem = ("sync", s_w) if k % 2 == 0 else ("scalar", s_w2)
            b.dma(eng, PRW[:, k, :], dsrc[k].rearrange("p m f -> p (m f)"), sem)
        b.wait("vector", s_w, s_w.c)
        b.wait("vector", s_w2, s_w2.c)
        if dbg_pfx:
            b.wait("sync", s_w, s_w.c)
            dump(dbg_pfx + "_prw0", PRW[:, 0, :])
        b.op("vector", lambda e: e.tensor_reduce(Fd[:, 0:1], PRW[:].rearrange("p k mf -> p (k mf)"),
                                                 AX, ALU.add, apply_absolute_value=True), inc=s_v)
        VA = s_v.c
        b.wait("tensor", s_v, VA)
        b.op("tensor", lambda e: e.transpose(PT[0:1, 0, 0:P], Fd[:, 0:1], IDE[:]), inc=s_t)
        TA = s_t.c
        b.wait("vector", s_t, TA)
        b.op("vector", lambda e: e.tensor_reduce(Sc1[0:1, 0:1], PT[0:1, 0, 0:P], AX, ALU.add), inc=s_v)
        b.drain("vector")
        b.op("vector", lambda e: e.tensor_scalar(Sc1[0:1, 0:1], Sc1[0:1, 0:1], 1.0 / (1024.0 * 1024.0), 1e-5,
                                                 ALU.mult, ALU.max), inc=s_v)
        b.drain("vector")
        b.op("vector", lambda e: e.tensor_scalar_mul(Sc1[0:1, 2:3], Sc1[0:1, 0:1], 1.0 / 127.0), inc=s_v)
        b.drain("vector", s_v)
        b.op("vector", lambda e: e.reciprocal(Sc1[0:1, 1:2], Sc1[0:1, 0:1]), inc=s_v)
        b.drain("vector", s_v)
        VB0 = s_v.c
        b.wait("tensor", s_v, VB0)
        b.op("tensor", lambda e: e.matmul(PB0[:, 0:1], OC[0:1, 0:P], Sc1[0:1, 1:2], start=True, stop=True), inc=s_t)
        b.op("tensor", lambda e: e.matmul(PB1[:, 0:1], OC[0:1, 0:P], Sc1[0:1, 2:3], start=True, stop=True), inc=s_t)
        TB = s_t.c
        b.wait("vector", s_t, TB)
        b.op("vector", lambda e: e.tensor_copy(SWB[:], PB0[:, 0:1]), inc=s_v)
        b.op("vector", lambda e, t=isw_t: e.tensor_copy(t[:], PB1[:, 0:1]), inc=s_v)
        b.drain("vector", s_v)
        if dbg_pfx:
            dump(dbg_pfx + "_swb", SWB[:])
        for k in range(CC):
            b.op("vector", lambda e, k=k: e.tensor_scalar_mul(PRS[:], PRW[:, k, :], SWB[:, 0:1]), inc=s_v)
            b.drain("vector")
            b.op("vector", lambda e: e.tensor_scalar(PRS[:], PRS[:], MAGIC, -MAGIC, ALU.add, ALU.add), inc=s_v)
            b.drain("vector", s_v)
            if dest_sb is not None:
                b.op("vector", lambda e, k=k: e.tensor_scalar(dest_sb[:, k, :], PRS[:], 1.0, -1.0,
                                                              ALU.min, ALU.max), inc=s_v)
            else:
                b.op("vector", lambda e: e.tensor_scalar(PRB[:], PRS[:], 1.0, -1.0, ALU.min, ALU.max), inc=s_v)
                VD = s_v.c
                b.wait("scalar", s_v, VD)
                b.op("scalar", lambda e, k=k: e.dma_start(
                    out=store_mmaj[:, :, k, :].rearrange("m p f -> p m f"),
                    in_=PRB[:].rearrange("p (m f) -> p m f", m=CC)), inc=s_pw, inc_n=16)
                b.wait("vector", s_pw, s_pw.c)

    prep_attn(dwk, WQK, ISW["k"], dbg_pfx="wk" if dbg else None)
    prep_attn(dwv, WQV, ISW["v"])
    prep_attn(dwr, WQR, ISW["r"])
    PREP_KVR_V = s_v.c
    prep_attn(dwo, None, ISW["o"], store_mmaj=wqo_h)
    WQO_N = s_pw.c
    b.wait("gpsimd", s_w, s_w.c)
    b.wait("gpsimd", s_w2, s_w2.c)
    for (src, dst, nchunk) in ((dw1, w1b, 2), (dw2, w2b, 2), (dh1, h1b, 8), (dh2, h2b, 8), (dw3, w3b, 4)):
        step = src.shape[0] // nchunk
        for i in range(nchunk):
            b.dma("gpsimd", dst[i * step:(i + 1) * step], src[i * step:(i + 1) * step], s_cast)
    CAST_W12, CAST_H1, CAST_H2, CAST_ALL = 16 * 4, 16 * 12, 16 * 20, 16 * 24
    dump("Dt", Dt[:]); dump("Eu", Eu[:]); dump("C1", C1[:]); dump("Rd", Rd[:])
    dump("iswk", ISW["k"][:]); dump("iswo", ISW["o"][:])
    dump("WQK", WQK[:]); dump("h0", H[:, :, 0])
    dump("DPs", DP[:, :, 0:8])

    def mmsum(lhsT_fn, bank, dst):
        ones = ONEC16 if lhsT_fn(0, 0).dtype == F16 else ONEC
        for blk in range(4):
            for cc in range(CC):
                b.op("tensor", lambda e, cc=cc, blk=blk, o=ones: e.matmul(
                    PT[:, bank, blk:blk + 1], lhsT_fn(cc, blk), o[:],
                    start=(cc == 0), stop=(cc == CC - 1)),
                    inc=(s_t if (blk == 3 and cc == CC - 1) else None))
        TS = s_t.c
        b.wait("vector", s_t, TS)
        b.op("vector", lambda e, d=dst: e.tensor_copy(d[:], PT[:, bank, 0:4]), inc=s_v)

    def tokpipe(srcU0, srcU1, red0, tinymath, lhs0, lhs1, mm0=None, mm1=None):
        if mm0 is not None:
            mmsum(mm0, 0, Tok0)
        if mm1 is not None:
            mmsum(mm1, 1, Tok1)
        if mm0 is None or mm1 is None:
            srcU0()
            b.drain("vector", s_v)
            VU = s_v.c
            b.wait("tensor", s_v, VU)
            for i in range(4):
                b.op("tensor", lambda e, i=i: e.transpose(PT[:, 0, i * P:(i + 1) * P],
                                                          U0[:, i * P:(i + 1) * P], IDE[:]),
                     inc=(s_t if i == 3 else None))
            TT = s_t.c
            b.wait("vector", s_t, TT)
            for i in range(4):
                b.op("vector", lambda e, i=i: e.tensor_reduce(Tok0[:, i:i + 1], PT[:, 0, i * P:(i + 1) * P],
                                                              AX, red0), inc=(s_v if i == 3 else None))
        b.drain("vector")
        tinymath()
        b.drain("vector", s_v)
        b.drain("scalar", s_s)
        b.wait("tensor", s_v, s_v.c)
        b.wait("tensor", s_s, s_s.c)
        for i in range(CC):
            b.op("tensor", lambda e, i=i: e.transpose(PT[0:1, i // 4, (i % 4) * P:(i % 4 + 1) * P],
                                                      Fd[:, i:i + 1], IDE[:]),
                 inc=(s_t if i == CC - 1 else None))
        TF = s_t.c
        b.wait("vector", s_t, TF)
        b.op("vector", lambda e: e.tensor_copy(FROW[:], PT[0:1, :, :].rearrange("p a t -> p (a t)")), inc=s_v)
        b.drain("vector", s_v)
        VG = s_v.c
        b.wait("tensor", s_v, VG)
        for i in range(4):
            b.op("tensor", lambda e, i=i: e.matmul(PB0[:, i * P:(i + 1) * P], lhs0, FROW[0:1, i * P:(i + 1) * P],
                                                   start=True, stop=True), inc=(s_t if i == 3 else None))
        for i in range(4):
            b.op("tensor", lambda e, i=i: e.matmul(PB1[:, i * P:(i + 1) * P], lhs1, FROW[0:1, (4 + i) * P:(5 + i) * P],
                                                   start=True, stop=True), inc=(s_t if i == 3 else None))
        b.wait("vector", s_t, s_t.c)

    def ln_math():
        b.op("vector", lambda e: e.tensor_scalar_mul(Fd[:, 0:4], Tok0[:], 1.0 / 1024.0), inc=s_v)
        b.drain("vector")
        b.op("vector", lambda e: e.tensor_mul(Tok0[:], Fd[:, 0:4], Fd[:, 0:4]), inc=s_v)
        b.op("vector", lambda e: e.tensor_scalar_mul(Tok1[:], Tok1[:], 1.0 / 1024.0), inc=s_v)
        b.drain("vector")
        b.op("vector", lambda e: e.tensor_sub(Tok1[:], Tok1[:], Tok0[:]), inc=s_v)
        b.drain("vector", s_v)
        VL = s_v.c
        b.wait("scalar", s_v, VL)
        b.op("scalar", lambda e: e.activation(Tok1[:], Tok1[:], AF.Ln, bias=CB[:, 0:1]), inc=s_s)
        b.drain("scalar")
        b.op("scalar", lambda e: e.activation(Fd[:, 4:8], Tok1[:], AF.Exp, scale=-0.5), inc=s_s)

    def quant_math(isw_t):
        b.op("vector", lambda e: e.tensor_scalar_max(Tok0[:], Tok0[:], 4e-5), inc=s_v)
        VQ = s_v.c
        b.wait("scalar", s_v, VQ)
        b.op("scalar", lambda e: e.activation(Fd[:, 0:4], Tok0[:], AF.Ln), inc=s_s)
        b.op("scalar", lambda e: e.activation(Fd[:, 0:4], Fd[:, 0:4], AF.Exp, scale=-1.0), inc=s_s)
        b.op("scalar", lambda e: e.activation(Tok1[:], Tok1[:], AF.Ln, scale=1.0 / 1024.0, bias=CB[:, 1:2]), inc=s_s)
        b.op("scalar", lambda e: e.activation(Tok1[:], Tok1[:], AF.Exp, scale=-0.5), inc=s_s)
        SQm = s_s.c
        b.wait("vector", s_s, SQm)
        b.op("vector", lambda e: e.tensor_mul(Fd[:, 4:8], Tok0[:], Tok1[:]), inc=s_v)
        b.op("vector", lambda e, t=isw_t: e.tensor_scalar_mul(Fd[:, 4:8], Fd[:, 4:8], t[:, 0:1]), inc=s_v)

    def emit_ln(x_t, out_h, sq_t, c_t):
        b.wait("scalar", s_v, s_v.c)
        b.wait("tensor", s_v, s_v.c)
        b.op("scalar", lambda e, x=x_t, s=sq_t: e.activation(s[:], x[:], AF.Square), inc=s_s)
        b.drain("scalar", s_s)
        SS = s_s.c
        b.wait("tensor", s_s, SS)
        tokpipe(None, None, ALU.add, ln_math, OC[0:1, 0:P], OC[0:1, 0:P],
                mm0=lambda cc, blk, x=x_t: x[:, cc, blk * P:(blk + 1) * P],
                mm1=lambda cc, blk, s=sq_t: s[:, cc, blk * P:(blk + 1) * P])
        b.op("vector", lambda e, x=x_t, c=c_t: e.tensor_sub(
            c[:], x[:], PB0[:].unsqueeze(1).broadcast_to((P, CC, SL))), inc=s_v)
        b.op("vector", lambda e, c=c_t, o=out_h: e.tensor_mul(
            o, c[:], PB1[:].unsqueeze(1).broadcast_to((P, CC, SL))), inc=s_v)
        b.drain("vector", s_v)
        b.wait("tensor", s_v, s_v.c)

    def quant_head(kin_t, rms_t, y2_t, sq_t):
        b.wait("scalar", s_v, s_v.c)
        b.wait("scalar", s_t, s_t.c)
        for cc in range(CC):
            b.op("scalar", lambda e, cc=cc, k=kin_t, y=y2_t, r=rms_t: e.activation(
                y[:, cc, :], k[:, cc, :], AF.Copy, scale=r[:, cc:cc + 1]), inc=(s_s if cc == CC - 1 else None))
        b.op("scalar", lambda e, k=kin_t, s=sq_t: e.activation(s[:], k[:], AF.Square), inc=s_s)
        b.drain("scalar", s_s)
        return s_s.c

    def quant_tail(SY, isw_t, y2_t, sq_t, z_t):
        b.wait("tensor", s_s, SY)

        def u0():
            b.wait("vector", s_s, SY)
            b.op("vector", lambda e, y=y2_t: e.tensor_reduce(U0[:], y[:].transpose([0, 2, 1]), AX, ALU.max,
                                                             apply_absolute_value=True), inc=s_v)

        tokpipe(u0, None, ALU.max, lambda: quant_math(isw_t), OC[0:1, P:2 * P], OC[0:1, 0:P],
                mm1=lambda cc, blk, s=sq_t: s[:, cc, blk * P:(blk + 1) * P])
        b.op("vector", lambda e, y=y2_t: e.tensor_mul(y[:], y[:], PB0[:].unsqueeze(1).broadcast_to((P, CC, SL))),
             inc=s_v)
        b.op("vector", lambda e: e.tensor_copy(DBC[:], PB1[:]), inc=s_v)
        b.op("vector", lambda e, y=y2_t, z=z_t: e.tensor_scalar(z[:], y[:], MAGIC, -MAGIC, ALU.add, ALU.add), inc=s_v)
        b.drain("vector", s_v)
        b.wait("tensor", s_v, s_v.c)

    def emit_quant(kin_t, rms_t, isw_t, y2_t, sq_t, z_t):
        SY = quant_head(kin_t, rms_t, y2_t, sq_t)
        quant_tail(SY, isw_t, y2_t, sq_t, z_t)

    def emit_mm1024(wq_ap_fn, z_t, evac, nK=CC):
        evc = {}
        for m in range(CC):
            if m >= 4:
                b.wait("tensor", s_v, evc[m - 4])
            bank = PK[:, m % 4, :]
            for k in range(nK):
                b.op("tensor", lambda e, m=m, k=k, bank=bank, z=z_t: e.matmul(
                    bank, wq_ap_fn(m, k), z[:, k, :], start=(k == 0), stop=(k == nK - 1)),
                    inc=(s_t if k == nK - 1 else None))
            TM = s_t.c
            b.wait("vector", s_t, TM)
            evac(m, bank)
            evc[m] = s_v.c
            if m == CC - 1:
                b.drain("vector", s_v)

    b.wait("tensor", s_v, PREP_KVR_V)
    for s in range(NS):
        b.wait("vector", s_x, s_x.c)
        b.wait("scalar", s_x, s_x.c)
        b.wait("tensor", s_x, s_x.c)
        if s == 0:
            b.wait("vector", s_v, VH0)
        emit_ln(X0, H[:, :, 1:SL + 1], A3, A2)
        if s == 0:
            dump("U0_ln", U0[:]); dump("U1_ln", U1[:])
            dump("Tok0_ln", Tok0[:]); dump("Tok1_ln", Tok1[:])
            dump("Fd_ln", Fd[:]); dump("FROW_ln", FROW[:]); dump("X0_s0", X0[:])
        VLN = s_v.c
        if s + 1 < NS:
            b.wait("sync", s_v, VLN)
            b.dma("sync", X0[:], dx[s + 1], s_x)
        b.op("vector", lambda e: e.tensor_sub(Zb[:], H[:, :, 1:SL + 1], H[:, :, 0:SL]), inc=s_v)
        VD_ = s_v.c

        projs = (("tmk", "rmsk", "k", WQK, ZK, A0),
                 ("tmv", "rmsv", "v", WQV, ZV, A1),
                 ("tmr", "rmsr", "r", WQR, ZR, A4))

        def emit_kin(pi):
            tm = projs[pi][0]
            b.wait("vector", s_s, s_s.c)
            for cc in range(CC):
                b.op("vector", lambda e, cc=cc, tm=tm: e.scalar_tensor_tensor(
                    A2[:, cc, :], Zb[:, cc, :], V[tm][:, cc:cc + 1], H[:, cc, 0:SL], ALU.mult, ALU.add),
                    inc=(s_v if cc == CC - 1 else None))
            b.drain("vector", s_v)

        emit_kin(0)
        SYs = {0: quant_head(A2, V[projs[0][1]], A4, A3)}
        for pi in range(3):
            tm, rms, isw, wq, z_t, kdst = projs[pi]
            if s > 0:
                b.wait("vector", s_sp, 16 * 3 * s)
                if pi == 2:
                    b.wait("scalar", s_sp, 16 * 3 * s)
            quant_tail(SYs[pi], ISW[isw], A4, A3, z_t)
            if pi < 2:
                emit_kin(pi + 1)
                SYs[pi + 1] = quant_head(A2, V[projs[pi + 1][1]], A4, A3)

            def evac(m, bank, kdst=kdst, pi=pi):
                if pi != 2:
                    b.op("vector", lambda e, m=m, bank=bank, kd=kdst: e.tensor_mul(
                        kd[:, m, 0:SL], bank, DBC[:]), inc=s_v)
                else:
                    b.op("vector", lambda e, m=m, bank=bank, kd=kdst: e.tensor_mul(
                        kd[:, m, :], bank, DBC[:]), inc=s_v)
            emit_mm1024(lambda m, k, wq=wq: wq[:, k, m * P:(m + 1) * P], z_t, evac)

        b.wait("scalar", s_v, s_v.c)
        b.op("scalar", lambda e: e.activation(F2[:], A4[:], AF.Sigmoid), inc=s_s)
        b.op("scalar", lambda e: e.activation(A3[:], A0[:, :, 0:SL], AF.Exp), inc=s_s)
        SEK = s_s.c
        b.wait("vector", s_s, SEK)
        b.op("vector", lambda e: e.tensor_mul(A4[:], A3[:], A1[:, :, 0:SL]), inc=s_v)
        if s == 0:
            b.wait("vector", s_g, MEMS)
        b.op("vector", lambda e: e.tensor_copy(A0[:, :, 0], PiT[:]), inc=s_v)
        b.op("vector", lambda e: e.tensor_copy(A1[:, :, 0], QiT[:]), inc=s_v)
        for cc in range(CC):
            b.op("vector", lambda e, cc=cc: e.tensor_tensor_scan(
                A0[:, cc, 1:SL + 1], Dt[:, cc:cc + 1].broadcast_to((P, SL)), A4[:, cc, :],
                PiT[:, cc:cc + 1], ALU.mult, ALU.add), inc=(s_v if cc == CC - 1 else None))
        for cc in range(CC):
            b.op("vector", lambda e, cc=cc: e.tensor_tensor_scan(
                A1[:, cc, 1:SL + 1], Dt[:, cc:cc + 1].broadcast_to((P, SL)), A3[:, cc, :],
                QiT[:, cc:cc + 1], ALU.mult, ALU.add), inc=(s_v if cc == CC - 1 else None))
        for cc in range(CC):
            b.op("vector", lambda e, cc=cc: e.scalar_tensor_tensor(
                F0[:, cc, :], A4[:, cc, :], Eu[:, cc:cc + 1], A0[:, cc, 0:SL], ALU.mult, ALU.add),
                inc=(s_v if cc == CC - 1 else None))
        for cc in range(CC):
            b.op("vector", lambda e, cc=cc: e.scalar_tensor_tensor(
                F1[:, cc, :], A3[:, cc, :], Eu[:, cc:cc + 1], A1[:, cc, 0:SL], ALU.mult, ALU.add),
                inc=(s_v if cc == CC - 1 else None))
        b.op("vector", lambda e: e.tensor_copy(PiT[:], A0[:, :, SL]), inc=s_v)
        b.op("vector", lambda e: e.tensor_copy(QiT[:], A1[:, :, SL]), inc=s_v)
        b.drain("vector", s_v)
        b.wait("vector", s_g, s_g.c)
        b.op("vector", lambda e: e.tensor_copy(H[:, :, 0], H[:, :, SL]), inc=s_v)
        VSP = s_v.c
        if s == 0:
            dump("H_s0", H[:]); dump("pst_s0", A0[:]); dump("qst_s0", A1[:])
            dump("ek_s0", A3[:]); dump("EV_s0", A4[:])
            dump("F0_s0", F0[:]); dump("F1_s0", F1[:]); dump("F2_s0", F2[:])
        b.wait("scalar", s_v, VSP)
        b.op("scalar", lambda e, s=s: e.dma_start(out=sp_n[s], in_=F0[:].rearrange("p c t -> p (c t)")),
             inc=s_sp, inc_n=16)
        b.op("scalar", lambda e, s=s: e.dma_start(out=sp_d[s], in_=F1[:].rearrange("p c t -> p (c t)")),
             inc=s_sp, inc_n=16)
        b.op("scalar", lambda e, s=s: e.dma_start(out=sp_r[s], in_=F2[:].rearrange("p c t -> p (c t)")),
             inc=s_sp, inc_n=16)

    b.drain("vector")
    b.op("vector", lambda e: e.tensor_copy(Gio[:, 0:8], PiT[:]), inc=s_v)
    b.op("vector", lambda e: e.tensor_copy(Gio[:, 8:16], QiT[:]), inc=s_v)
    b.drain("vector", s_v)
    VCC = s_v.c
    b.wait("gpsimd", s_v, VCC)
    b.dma("gpsimd", cc_in, Gio[:], s_cc)
    b.wait("gpsimd", s_cc, 16)
    b.op("gpsimd", lambda e: e.collective_compute(
        "AllGather", ALU.bypass, replica_groups=[[0, 1], [2, 3], [4, 5], [6, 7]],
        ins=[cc_in], outs=[cc_out]), inc=s_cc)
    b.wait("gpsimd", s_cc, 17)
    b.dma("gpsimd", Gin[:], cc_out[0:P, :], s_cc)
    b.wait("vector", s_cc, 33)
    b.op("vector", lambda e: e.tensor_mul(Pcs[:], Gin[:, 0:8], Msk[:]), inc=s_v)
    b.op("vector", lambda e: e.tensor_mul(Qcs[:], Gin[:, 8:16], Msk[:]), inc=s_v)
    b.drain("vector", s_v)
    VCM = s_v.c

    b.wait("scalar", s_v, VCM)
    b.wait("scalar", s_sp, 16 * 3 * NS)
    b.op("scalar", lambda e: e.dma_start(out=F0[:].rearrange("p c t -> p (c t)"), in_=sp_n[0]), inc=s_rl, inc_n=16)
    b.op("scalar", lambda e: e.dma_start(out=F1[:].rearrange("p c t -> p (c t)"), in_=sp_d[0]), inc=s_rl, inc_n=16)
    b.op("scalar", lambda e: e.dma_start(out=F2[:].rearrange("p c t -> p (c t)"), in_=sp_r[0]), inc=s_rl, inc_n=16)
    b.wait("sync", s_v, VCM)
    b.dma("sync", X0[:], dx[0], s_x)

    free_at = {}

    def stream_load(eng, buf, src_ap, sem):
        key = buf.name
        if key in free_at:
            sem2, cnt = free_at[key]
            b.wait(eng, sem2, cnt)
        b.dma(eng, buf[:].rearrange("p k f -> p (k f)"), src_ap, sem)

    def b2_early1(s):
        b.wait("vector", s_rl, 48 * (s + 1))
        if s > 0:
            b.op("vector", lambda e: e.tensor_mul(Pcs[:], Pcs[:], C1[:]), inc=s_v)
            b.op("vector", lambda e: e.tensor_mul(Qcs[:], Qcs[:], C1[:]), inc=s_v)
            b.drain("vector", s_v)
        for cc in range(CC):
            b.op("vector", lambda e, cc=cc: e.scalar_tensor_tensor(
                F0[:, cc, :], DP[:, cc, :], Pcs[:, cc:cc + 1], F0[:, cc, :], ALU.mult, ALU.add),
                inc=(s_v if cc == CC - 1 else None))
        for cc in range(CC):
            b.op("vector", lambda e, cc=cc: e.scalar_tensor_tensor(
                F1[:, cc, :], DP[:, cc, :], Qcs[:, cc:cc + 1], F1[:, cc, :], ALU.mult, ALU.add),
                inc=(s_v if cc == CC - 1 else None))
        b.drain("vector", s_v)
        VCR = s_v.c
        b.wait("scalar", s_v, VCR)
        b.wait("scalar", s_t, s_t.c)
        b.op("scalar", lambda e: e.activation(QS[:], F1[:], AF.Ln), inc=s_s)
        b.drain("scalar", s_s)
        b.op("scalar", lambda e: e.activation(F1[:], QS[:], AF.Exp, scale=-1.0), inc=s_s)
        b.drain("scalar", s_s)
        SIV = s_s.c
        b.wait("vector", s_s, SIV)
        b.op("vector", lambda e: e.tensor_mul(QS[:], F0[:], F1[:]), inc=s_v)
        b.op("vector", lambda e: e.tensor_mul(QS[:], QS[:], F2[:]), inc=s_v)
        b.drain("vector", s_v)
        return quant_head(QS, V["rmso"], Y2O, SQO)

    def b2_early2(SY):
        quant_tail(SY, ISW["o"], Y2O, SQO, ZK)

    def b2_late(s):
        if s == 0:
            b.wait("sync", s_pw, WQO_N)
            b.wait("scalar", s_pw, WQO_N)

        def oq(m):
            return ("sync", s_fh) if m % 2 == 0 else ("scalar", s_fw)

        LWm = {}
        for m in (0, 1):
            eng, sem = oq(m)
            stream_load(eng, FQ0 if m % 2 == 0 else FQ1,
                        wqo_h[m].rearrange("p k f -> p (k f)"), sem)
            LWm[m] = (sem, sem.c)
        for m in range(CC):
            buf = FQ0 if m % 2 == 0 else FQ1
            sem, cnt = LWm[m]
            b.wait("tensor", sem, cnt)
            i = m % 4
            if i in bank_free:
                sem2, cnt2 = bank_free[i]
                b.wait("tensor", sem2, cnt2)
            bank = PK[:, i, :]
            for k in range(CC):
                b.op("tensor", lambda e, k=k, bank=bank, buf=buf: e.matmul(
                    bank, buf[:, k, :], ZK[:, k, :], start=(k == 0), stop=(k == CC - 1)),
                    inc=(s_t if k == CC - 1 else None))
            free_at[buf.name] = (s_t, s_t.c)
            if m + 2 < CC:
                eng, sem3 = oq(m)
                stream_load(eng, buf, wqo_h[m + 2].rearrange("p k f -> p (k f)"), sem3)
                LWm[m + 2] = (sem3, sem3.c)
            b.wait("vector", s_t, s_t.c)
            b.op("vector", lambda e, m=m, bank=bank: e.tensor_mul(QS[:, m, :], bank, DBC[:]), inc=s_v)
            bank_free[i] = (s_v, s_v.c)
        if s + 1 < NS:
            b.wait("scalar", s_t, s_t.c)
            b.wait("scalar", s_v, s_v.c)
            b.op("scalar", lambda e, s=s: e.dma_start(out=F0[:].rearrange("p c t -> p (c t)"), in_=sp_n[s + 1]),
                 inc=s_rl, inc_n=16)
            b.op("scalar", lambda e, s=s: e.dma_start(out=F1[:].rearrange("p c t -> p (c t)"), in_=sp_d[s + 1]),
                 inc=s_rl, inc_n=16)
            b.op("scalar", lambda e, s=s: e.dma_start(out=F2[:].rearrange("p c t -> p (c t)"), in_=sp_r[s + 1]),
                 inc=s_rl, inc_n=16)
        b.op("vector", lambda e: e.tensor_add(X0[:], X0[:], QS[:]), inc=s_v)
        if s == 0 and dbg:
            dump("xp_s0", X0[:])
        emit_ln(X0, H[:, :, 0:SL], QS, YF)
        if s == 0 and dbg:
            dump("g_s0", H[:, :, 0:SL])

    def b2_ffn_w12(s):
        VG2 = s_v.c
        svlast = 0
        b.wait("sync", s_cast, CAST_W12)
        b.wait("scalar", s_cast, CAST_W12)
        LW1, LW2 = {}, {}
        for m in (0, 1):
            stream_load("sync", FW2 if m % 2 == 0 else FW3, w1b[m].rearrange("p k f -> p (k f)"), s_fh)
            LW1[m] = s_fh.c
            stream_load("scalar", FW0 if m % 2 == 0 else FW1, w2b[m].rearrange("p k f -> p (k f)"), s_fw)
            LW2[m] = s_fw.c
        for m in range(MH):
            if m == 0:
                b.wait("tensor", s_v, VG2)
                b.wait("tensor", s_cast, CAST_W12)
            b.wait("tensor", s_fh, LW1[m])
            b.wait("tensor", s_fw, LW2[m])
            w1t = FW2 if m % 2 == 0 else FW3
            w2t = FW0 if m % 2 == 0 else FW1
            ia, ib = (2 * m) % 4, (2 * m + 1) % 4
            for i in (ia, ib):
                if i in bank_free:
                    sem2, cnt = bank_free[i]
                    b.wait("tensor", sem2, cnt)
            bank_a, bank_b = PK[:, ia, :], PK[:, ib, :]
            for k in range(CC):
                b.op("tensor", lambda e, k=k, bank=bank_a, w=w1t: e.matmul(
                    bank, w[:, k, :], H[:, k, 0:SL], start=(k == 0), stop=(k == CC - 1)),
                    inc=(s_t if k == CC - 1 else None))
            TA1 = s_t.c
            for k in range(CC):
                b.op("tensor", lambda e, k=k, bank=bank_b, w=w2t: e.matmul(
                    bank, w[:, k, :], H[:, k, 0:SL], start=(k == 0), stop=(k == CC - 1)),
                    inc=(s_t if k == CC - 1 else None))
            TA2 = s_t.c
            free_at[w1t.name] = (s_t, TA1)
            free_at[w2t.name] = (s_t, TA2)
            if m + 2 < MH:
                stream_load("sync", w1t, w1b[m + 2].rearrange("p k f -> p (k f)"), s_fh)
                LW1[m + 2] = s_fh.c
                stream_load("scalar", w2t, w2b[m + 2].rearrange("p k f -> p (k f)"), s_fw)
                LW2[m + 2] = s_fw.c
            b.wait("scalar", s_t, TA1)
            b.wait("scalar", s_v, svlast)
            b.op("scalar", lambda e, m=m, bank=bank_a: e.activation(
                ABLK[:], bank, AF.Silu, scale=S1t[:, m:m + 1]), inc=s_s)
            SA = s_s.c
            bank_free[ia] = (s_s, SA)
            b.wait("vector", s_t, TA2)
            b.wait("vector", s_s, SA)
            b.op("vector", lambda e, m=m, bank=bank_b: e.scalar_tensor_tensor(
                HB[:, m, :], bank, S2t[:, m:m + 1], ABLK[:], ALU.mult, ALU.mult), inc=s_v)
            svlast = s_v.c
            bank_free[ib] = (s_v, svlast)
        b.drain("vector", s_v)

    def ffn_big(src_t, dst_t, wsrc, scl, cast_need, nM=MH, gelu=True, wait_v=None, wait_s=None):
        b.wait("sync", s_cast, cast_need)
        b.wait("scalar", s_cast, cast_need)
        if wait_v is not None:
            b.wait("tensor", s_v, wait_v)
        if wait_s is not None:
            b.wait("tensor", s_s, wait_s)

        def hq(m):
            return ("sync", s_fh) if m % 2 == 0 else ("scalar", s_fw)

        LH = {}
        for m in (0, 1):
            if m >= nM:
                break
            eng, sem = hq(m)
            stream_load(eng, FH0 if m % 2 == 0 else FH1, wsrc[m].rearrange("p k f -> p (k f)"), sem)
            LH[m] = (sem, sem.c)
        for m in range(nM):
            buf = FH0 if m % 2 == 0 else FH1
            sem, cnt = LH[m]
            b.wait("tensor", sem, cnt)
            i = m % 4
            if i in bank_free:
                sem2, cnt2 = bank_free[i]
                b.wait("tensor", sem2, cnt2)
            bank = PK[:, i, :]
            for k in range(MH):
                b.op("tensor", lambda e, k=k, bank=bank, buf=buf, src=src_t: e.matmul(
                    bank, buf[:, k, :], src[:, k, :], start=(k == 0), stop=(k == MH - 1)),
                    inc=(s_t if k == MH - 1 else None))
            TH = s_t.c
            free_at[buf.name] = (s_t, TH)
            if m + 2 < nM:
                eng, sem3 = hq(m)
                stream_load(eng, buf, wsrc[m + 2].rearrange("p k f -> p (k f)"), sem3)
                LH[m + 2] = (sem3, sem3.c)
            b.wait("scalar", s_t, TH)
            if not gelu:
                b.op("scalar", lambda e, m=m, bank=bank, dst=dst_t, scl=scl: e.activation(
                    dst[:, m, :], bank, AF.Copy, scale=scl[:, m:m + 1]), inc=s_s)
            else:
                b.op("scalar", lambda e, m=m, bank=bank, dst=dst_t, scl=scl: e.activation(
                    dst[:, m, :], bank, AF.Gelu, scale=scl[:, m:m + 1]), inc=s_s)
            bank_free[i] = (s_s, s_s.c)

    SYq = b2_early1(0)
    b2_early2(SYq)
    bank_free = {}
    for s in range(NS):
        b.wait("vector", s_x, s_x.c)
        b.wait("scalar", s_x, s_x.c)
        b.wait("tensor", s_x, s_x.c)
        b2_late(s)
        b2_ffn_w12(s)
        w12v = s_v.c
        if s + 1 < NS:
            SYq = b2_early1(s + 1)
        ffn_big(HB, H2B, h1b, Sh1t, CAST_H1, wait_v=w12v)
        h1s = s_s.c
        if s + 1 < NS:
            b2_early2(SYq)
        ffn_big(H2B, HB, h2b, Sh2t, CAST_H2, wait_s=h1s)
        h2s = s_s.c
        ffn_big(HB, YF, w3b, S3t, CAST_ALL, nM=CC, gelu=False, wait_s=h2s)
        SW3 = s_s.c
        b.wait("vector", s_s, SW3)
        b.op("vector", lambda e: e.tensor_add(YF[:], YF[:], X0[:]), inc=s_v)
        b.drain("vector", s_v)
        VOUT = s_v.c
        b.wait("gpsimd", s_v, VOUT)
        b.op("gpsimd", lambda e, s=s: e.dma_start(out=dy[s], in_=YF[:]),
             inc=s_out, inc_n=16)
        OUTC = s_out.c
        if s + 1 < NS:
            b.wait("sync", s_v, VOUT)
            b.dma("sync", X0[:], dx[s + 1], s_x)
        b.wait("vector", s_out, OUTC)
        b.wait("scalar", s_out, OUTC)

    b.wait("gpsimd", s_out, s_out.c)
    b.wait("gpsimd", s_sp, 16 * 3 * NS)

    blk = nc.Block()
    blk.__enter__()
    for eng in B.ENGS:
        getattr(blk, eng)(lambda e, eng=eng: [f(e) for f in b.ops[eng]])
    blk.__exit__(None, None, None)
    return nc


def _host_prepare(inputs):
    x = np.ascontiguousarray(np.asarray(inputs["L0_attn_hidden"], dtype=np.float32))

    def tile_kpmf(w, KB, MB):
        return np.ascontiguousarray(np.asarray(w, np.float32).reshape(MB, P, KB, P).transpose(2, 3, 0, 1))

    def tile_mpkf(w, MB, KB):
        return np.ascontiguousarray(np.asarray(w, np.float32).reshape(MB, P, KB, P).transpose(0, 3, 2, 1))

    def vec_pc(v):
        return np.ascontiguousarray(np.asarray(v, np.float32).reshape(-1)[:1024].reshape(CC, P).T)

    def vec_pm(v, MB):
        return np.ascontiguousarray(np.asarray(v, np.float32).reshape(MB, P).T)

    common = dict(
        wk=tile_kpmf(inputs["L0_attn_w_key"], CC, CC),
        wv=tile_kpmf(inputs["L0_attn_w_value"], CC, CC),
        wr=tile_kpmf(inputs["L0_attn_w_receptance"], CC, CC),
        wo=tile_kpmf(inputs["L0_attn_w_output"], CC, CC),
        w1=tile_mpkf(inputs["L0_ffn_w1_w"], MH, CC),
        w2=tile_mpkf(inputs["L0_ffn_w2_w"], MH, CC),
        h1=tile_mpkf(inputs["L0_ffn_h1_w"], MH, MH),
        h2=tile_mpkf(inputs["L0_ffn_h2_w"], MH, MH),
        w3=tile_mpkf(inputs["L0_ffn_w3_w"], CC, MH),
    )
    vp0 = np.zeros((P, 320), np.float32)
    vnames = ("L0_attn_time_mix_key", "L0_attn_time_mix_value", "L0_attn_time_mix_receptance",
              "L0_attn_rms_key", "L0_attn_rms_value", "L0_attn_rms_receptance",
              "L0_attn_rms_output", "L0_attn_time_decay", "L0_attn_time_first")
    for i, n in enumerate(vnames):
        vp0[:, 16 * i:16 * i + CC] = vec_pc(inputs[n])
    vp0[:, 176:176 + MH] = vec_pm(inputs["L0_ffn_w1_s"], MH)
    vp0[:, 208:208 + MH] = vec_pm(inputs["L0_ffn_w2_s"], MH)
    vp0[:, 240:240 + MH] = vec_pm(inputs["L0_ffn_h1_s"], MH)
    vp0[:, 272:272 + MH] = vec_pm(inputs["L0_ffn_h2_s"], MH)
    vp0[:, 304:304 + CC] = vec_pm(inputs["L0_ffn_w3_s"], CC)
    in_maps = []
    for c in range(8):
        bb, half = c // 2, c % 2
        xT = x[bb, half * TOK:(half + 1) * TOK, :].T
        xs = np.ascontiguousarray(xT.reshape(CC, P, NS, SL).transpose(2, 1, 0, 3))
        vp = vp0.copy()
        if half == 1:
            vp[:, 144:144 + CC] = 1.0
            vp[:, 160:160 + CC] = vec_pc(x[bb, TOK - 1, :])
        im = dict(common)
        im["x"] = xs
        im["vpack"] = vp
        in_maps.append(im)
    return in_maps


_NC_CACHE = {}


def kernel(**inputs):
    in_maps = _host_prepare(inputs)
    if "nc" not in _NC_CACHE:
        _NC_CACHE["nc"] = build_nc()
    nc = _NC_CACHE["nc"]
    res = run_bass_kernel_spmd(nc, in_maps, core_ids=list(range(8)))
    out = np.empty((4, 8192, 1024), np.float32)
    for c in range(8):
        bb, half = c // 2, c % 2
        ys = res.results[c]["y"]
        out[bb, half * TOK:(half + 1) * TOK, :] = ys.transpose(0, 3, 2, 1).reshape(TOK, 1024)
    return out
